# version 09
# baseline (speedup 1.0000x reference)
import contextlib
import numpy as np
import concourse.bass as bass
import concourse.mybir as mybir
from concourse.bass_utils import run_bass_kernel_spmd

F32 = mybir.dt.float32
BF16 = mybir.dt.bfloat16
AF = mybir.ActivationFunctionType
ALU = mybir.AluOpType
AX = mybir.AxisListType

B, S, D, H, DH, E, M = 2, 8192, 2048, 16, 128, 8, 2048
ROWS = 2048
NQT = S // 512
SCALE = float(np.float32(1.0) / np.float32(np.sqrt(np.float32(128.0))))
CEXP = 12.0
NMAX = [12, 24, 36, 48]
CUM = [0, 12, 36, 72]
NREST = 120
EPS = 1e-5
ENGS = ("sync", "gpsimd", "scalar", "vector", "tensor")
P3_LEVEL = 4

DMA_SEMS = ("dmc", "dmg", "dx0", "dx1", "lw", "wr0", "wr1", "spill", "sp0", "sp1", "csld",
            "hd0", "hd1", "st0", "st1", "st2", "st3", "stv", "stln", "stx", "sto",
            "xb0", "xb1")


def own_chunks(j):
    return sorted({j, 7 - j, 8 + j, 15 - j})


def slot_units(s):
    units = []
    for t in range(4 * s):
        units.append(("own", t, None))
    for dk in range(4):
        units.append(("diag", 4 * s + dk, dk))
    for r in range(NMAX[s]):
        units.append(("rest", 16 + r, CUM[s] + r))
    return units


class Sch:

    def __init__(self, nc):
        self.nc = nc
        self.stack = contextlib.ExitStack()
        self.sem = {}
        self.cnt = {}
        self.q = None

    def sems(self, *names):
        for n in names:
            self.sem[n] = self.stack.enter_context(self.nc.semaphore(n))
            self.cnt[n] = 0

    def begin(self):
        assert self.q is None
        self.q = {e: [] for e in ENGS}

    def add(self, eng, fn, waits=(), inc=None, by=1, auto=True):
        wd = {}
        for it in waits:
            if isinstance(it, tuple):
                nm, val = it
            else:
                nm, val = it, self.cnt[it]
            wd[nm] = max(wd.get(nm, 0), val)
        if auto and eng == "vector":
            wd["dve"] = max(wd.get("dve", 0), self.cnt["dve"])
        if auto and eng == "scalar":
            wd["act"] = max(wd.get("act", 0), self.cnt["act"])
        w = [(nm, val) for nm, val in wd.items() if val > 0]
        if inc is not None:
            self.cnt[inc] += by
        self.q[eng].append((fn, w, inc, by))

    def dma(self, eng, out, in_, waits=(), inc=None, by=16, **kw):
        assert inc is not None
        self.add(eng, lambda e: e.dma_start(out, in_, **kw), waits, inc, by, auto=False)

    def wait_only(self, eng, waits):
        self.add(eng, lambda e: None, waits, None)

    def drain(self):
        self.wait_only("gpsimd", [(n, self.cnt[n]) for n in DMA_SEMS])

    def emit(self, name):
        q, self.q = self.q, None
        sem = self.sem

        def mk(eng_name):
            ops = q[eng_name]

            def body(e):
                for fn, w, inc, by in ops:
                    for sname, val in w:
                        e.wait_ge(sem[sname], val)
                    ins = fn(e)
                    if inc is not None:
                        assert ins is not None
                        ins.then_inc(sem[inc], by)
            return body
        with self.nc.Block(name) as blk:
            if q["gpsimd"]:
                blk.gpsimd(mk("gpsimd"))
            if q["scalar"]:
                blk.scalar(mk("scalar"))
            if q["tensor"]:
                blk.tensor(mk("tensor"))
            if q["vector"]:
                blk.vector(mk("vector"))
            if q["sync"]:
                blk.sync(mk("sync"))


def build_program(phases=("k", "q", "v", "att", "oproj", "ffn"), dbg=None):
    nc = bass.Bass("TRN2", target_bir_lowering=False)
    sch = Sch(nc)
    sb = lambda name, shape, dt: sch.stack.enter_context(nc.sbuf_tensor(name, shape, dt))

    x = nc.dram_tensor("x", [S, D], F32, kind="ExternalInput")
    csA_d = nc.dram_tensor("csA", [128, S], F32, kind="ExternalInput")
    csB_d = nc.dram_tensor("csB", [128, S], F32, kind="ExternalInput")
    wqT = nc.dram_tensor("wqT", [D, D], F32, kind="ExternalInput")
    wkT = nc.dram_tensor("wkT", [D, D], F32, kind="ExternalInput")
    wvT = nc.dram_tensor("wvT", [D, D], F32, kind="ExternalInput")
    woT = nc.dram_tensor("woT", [D, D], F32, kind="ExternalInput")
    bqr = nc.dram_tensor("bqr", [128, 16], F32, kind="ExternalInput")
    bkr = nc.dram_tensor("bkr", [128, 16], F32, kind="ExternalInput")
    bvr_d = nc.dram_tensor("bvr", [128, 16], F32, kind="ExternalInput")
    bo_bc_d = nc.dram_tensor("bo_bc", [128, D], F32, kind="ExternalInput")
    gate_r = nc.dram_tensor("gate_r", [128, 128], F32, kind="ExternalInput")
    wi0 = nc.dram_tensor("wi0", [E, D, M], F32, kind="ExternalInput")
    wi1 = nc.dram_tensor("wi1", [E, D, M], F32, kind="ExternalInput")
    wof = nc.dram_tensor("wof", [E, M, D], F32, kind="ExternalInput")
    sb_sc_d = nc.dram_tensor("sb_scale", [128, NREST], F32, kind="ExternalInput")
    sb_bi_d = nc.dram_tensor("sb_bias", [128, NREST], F32, kind="ExternalInput")
    masks_d = nc.dram_tensor("masks", [128, 4 * 512], F32, kind="ExternalInput")
    ident_d = nc.dram_tensor("ident", [128, 128], F32, kind="ExternalInput")
    ones_d = nc.dram_tensor("ones_in", [2, 128], F32, kind="ExternalInput")
    constv = nc.dram_tensor("constv", [128, 8], F32, kind="ExternalInput")
    out = nc.dram_tensor("out", [ROWS, D], F32, kind="ExternalOutput")
    dbg_out = None
    if dbg is not None:
        dbg_out = nc.dram_tensor("dbg", list(dbg[1]), dbg[2], kind="ExternalOutput")

    ln1T_d = nc.dram_tensor("ln1T_d", [NQT, D, 512], BF16, kind="Internal")
    q_d = nc.dram_tensor("q_d", [H, 128, ROWS], BF16, kind="Internal")
    k_d = nc.dram_tensor("k_d", [H, 128, S], BF16, kind="Internal")
    v_d = nc.dram_tensor("v_d", [H, 128, 64, 128], BF16, kind="Internal")
    ln2T_d = nc.dram_tensor("ln2T_d", [4, D, 512], BF16, kind="Internal")
    xp_d = nc.dram_tensor("xp_d", [ROWS, D], F32, kind="Internal")
    c_d = nc.dram_tensor("c_d", [128, H * ROWS], BF16, kind="Internal")
    cw_d = nc.dram_tensor("cw_d", [128, 128], F32, kind="Internal")

    sch.sems(*DMA_SEMS)
    sch.sems("pe", "act", "dve")

    ident_bf = sb("ident_bf", [128, 128], BF16)
    ones_bf = sb("ones_bf", [128, 1], BF16)
    ones1f = sb("ones1f", [1, 128], F32)
    cw_sb = sb("cw_sb", [128, 16 * 8], F32)
    cst = sb("cst", [128, 8], F32)

    sch.begin()
    sch.dma("gpsimd", ident_bf[:, :], ident_d[:, :], inc="dmg")
    sch.dma("gpsimd", ones_bf[:, :], ones_d[0:1, :].rearrange("c p -> p c"), inc="dmg",
            allow_slow_non_contiguous=True)
    sch.dma("sync", ones1f[0:1, :], ones_d[1:2, :], inc="dmc")
    sch.dma("sync", cst[:, :], constv[:, :], inc="dmc")
    sch.drain()
    sch.emit("consts")

    p1s = contextlib.ExitStack()
    sb1 = lambda n, shp, dt: p1s.enter_context(nc.sbuf_tensor(n, shp, dt))
    ps1 = lambda n, shp, dt=F32: p1s.enter_context(nc.psum_tensor(n, shp, dt))
    wsb = sb1("p1_w", [128, 16 * 2048], BF16)
    lnT = [sb1(f"p1_lnT{i}", [128, 16 * 1024], BF16) for i in range(2)]
    xb = sb1("p1_x", [128, 2 * 2048], F32)
    lnbf = [sb1(f"p1_lnbf{i}", [128, 2048], BF16) for i in range(8)]
    scr = sb1("p1_scr", [128, 2048], F32)
    stat = sb1("p1_stat", [128, 16], F32)
    csAp = sb1("p1_csAp", [128, 1024], F32)
    csBp = sb1("p1_csBp", [128, 1024], F32)
    t1 = sb1("p1_t1", [128, 512], F32)
    t2 = sb1("p1_t2", [128, 512], F32)
    t3 = sb1("p1_t3", [128, 512], F32)
    stg = sb1("p1_stg", [128, 2 * 512], BF16)
    bcol = sb1("p1_b", [128, 16], F32)
    bvb = scr
    mm4 = [ps1(f"p1_mm{i}", [128, 512]) for i in range(4)]
    psT = [ps1(f"p1_psT{i}", [128, 128], BF16) for i in range(2)]

    def load_w(wdram):
        for dc in range(16):
            sch.dma("gpsimd", wsb[:, dc * 2048:(dc + 1) * 2048],
                    wdram[dc * 128:(dc + 1) * 128, :], inc="lw")
        return sch.cnt["lw"]

    def ln_stats(pair, i, st):
        row = pair * 1024 + i * 128
        sl = i % 2
        xslot = xb[:, sl * 2048:(sl + 1) * 2048]
        dxs = f"dx{sl}"
        w = []
        if len(st["norm"]) >= 2:
            w.append(("dve", st["norm"][-2]))
        sch.dma("sync", xslot, x[row:row + 128, :], waits=w, inc=dxs)
        ldx = (dxs, sch.cnt[dxs])
        sch.add("scalar", lambda e, xs=xslot: e.activation(scr[:, :], xs, AF.Copy, accum_out=stat[:, 0:1]),
                waits=[ldx, ("dve", sch.cnt["dve"])], inc="act")
        sch.add("scalar", lambda e, xs=xslot: e.activation(scr[:, :], xs, AF.Square, accum_out=stat[:, 1:2]),
                waits=[], inc="act")
        sch.add("vector", lambda e: e.tensor_scalar(stat[:, 2:3], stat[:, 0:1], 1.0 / 2048.0, None, ALU.mult),
                waits=["act"], inc="dve")
        sch.add("vector", lambda e: e.tensor_tensor(stat[:, 3:4], stat[:, 2:3], stat[:, 2:3], ALU.mult),
                waits=["dve"], inc="dve")
        sch.add("vector", lambda e: e.tensor_scalar(stat[:, 4:5], stat[:, 1:2], 1.0 / 2048.0, None, ALU.mult),
                waits=["dve"], inc="dve")
        sch.add("vector", lambda e: e.tensor_tensor(stat[:, 5:6], stat[:, 4:5], stat[:, 3:4], ALU.subtract),
                waits=["dve"], inc="dve")
        sch.add("scalar", lambda e: e.activation(stat[:, 6:7], stat[:, 5:6], AF.Sqrt, bias=cst[:, 0:1]),
                waits=["dve"], inc="act")
        sch.add("vector", lambda e: e.reciprocal(stat[:, 7:8], stat[:, 6:7]),
                waits=["act"], inc="dve")
        sch.add("vector", lambda e: e.scalar_tensor_tensor(stat[:, 8:9], stat[:, 2:3], -1.0, stat[:, 7:8], ALU.mult, ALU.mult),
                waits=["dve"], inc="dve")
        w = ["dve"]
        if i in st["evacT"]:
            w.append(("pe", st["evacT"][i]))
        lnslot = lnbf[i]
        sch.add("vector", lambda e, xs=xslot, lnslot=lnslot: e.tensor_scalar(lnslot[:, :], xs, stat[:, 7:8], stat[:, 8:9], ALU.mult, ALU.add),
                waits=w, inc="dve")
        st["norm"].append(sch.cnt["dve"])
        st["lnr"][i] = sch.cnt["dve"]

    def ln_transpose(pair, i, lnT_free_pe, st):
        lb = lnT[pair % 2]
        lnslot = lnbf[i]
        lnT_free_st = sch.cnt["stln"]
        for dc in range(16):
            bk = dc % 2
            w = [("dve", st["lnr"][i])]
            if ("ev", bk) in st["evacT"]:
                w.append(st["evacT"][("ev", bk)])
            sch.add("tensor", lambda e, bk=bk, dc=dc, lnslot=lnslot: e.transpose(
                    psT[bk][:, :], lnslot[:, dc * 128:(dc + 1) * 128], ident_bf[:, :]),
                    waits=w, inc="pe")
            ew = [("pe", sch.cnt["pe"]), ("stln", lnT_free_st)]
            if lnT_free_pe > 0:
                ew.append(("pe", lnT_free_pe))
            sch.add("scalar", lambda e, bk=bk, dc=dc, i=i, lb=lb: e.activation(
                    lb[:, dc * 1024 + i * 128: dc * 1024 + i * 128 + 128], psT[bk][:, :], AF.Copy),
                    waits=ew, inc="act")
            st["evacT"][("ev", bk)] = ("act", sch.cnt["act"])
        st["evacT"][i] = sch.cnt["pe"]
        if i == 7:
            done = ("act", sch.cnt["act"])
            for half in range(2):
                qt = 2 * pair + half
                src = lb.ap().rearrange("p (c t) -> p c t", t=1024)[:, :, half * 512:(half + 1) * 512]
                dst = ln1T_d.ap()[qt].rearrange("(c p) t -> p c t", p=128)
                sch.dma("scalar", dst, src, waits=[done], inc="stln")
            st["done"][pair] = done

    def load_lnT(pair, free_pe):
        st0 = sch.cnt["stln"]
        lb = lnT[pair % 2]
        sps = f"sp{pair % 2}"
        for half in range(2):
            qt = 2 * pair + half
            dst = lb.ap().rearrange("p (c t) -> p c t", t=1024)[:, :, half * 512:(half + 1) * 512]
            src = ln1T_d.ap()[qt].rearrange("(c p) t -> p c t", p=128)
            w = [("pe", free_pe), ("stln", st0)] if free_pe > 0 else [("stln", st0)]
            sch.dma("sync", dst, src, waits=w, inc=sps)
        return (sps, sch.cnt[sps])

    def rope_evac(bank, h, pair, half, out_ap, stg_hist, rope_done, tabs):
        sl = h % 2
        sts = f"st{sl}"
        c0 = half * 512
        pev = sch.cnt["pe"]
        w0 = [("pe", pev), ("csld", tabs)]
        if stg_hist.get(sl, 0) > 0:
            w0.append((sts, stg_hist[sl]))
        sch.add("vector", lambda e: e.scalar_tensor_tensor(t1[:, :], bank[:, :], bcol[:, h:h + 1], csAp[:, c0:c0 + 512], ALU.add, ALU.mult), waits=w0, inc="dve")
        sch.add("vector", lambda e: e.scalar_tensor_tensor(t2[:, :], bank[:, :], bcol[:, h:h + 1], csBp[:, c0:c0 + 512], ALU.add, ALU.mult), inc="dve")
        rope_done[id(bank)] = sch.cnt["dve"]
        sch.add("vector", lambda e: e.tensor_copy(t3[0:64, :], t2[64:128, :]), waits=["dve"], inc="dve")
        sch.add("vector", lambda e: e.tensor_copy(t3[64:128, :], t2[0:64, :]), inc="dve")
        sch.add("vector", lambda e, sl=sl: e.tensor_tensor(stg[:, sl * 512:(sl + 1) * 512], t1[:, :], t3[:, :], ALU.add), waits=["dve"], inc="dve")
        sch.dma("scalar", out_ap, stg[:, sl * 512:(sl + 1) * 512], waits=["dve"], inc=sts)
        stg_hist[sl] = sch.cnt[sts]

    def proj_dmajor(out_dram, pairs, compute_ln):
        st = {"norm": [], "evacT": {}, "lnr": {}, "done": {}}
        stg_hist, rope_done = {}, {}
        lw = sch.cnt["lw"]
        mm_done = {}
        rope_last = {}
        if compute_ln:
            for i in range(8):
                ln_stats(pairs[0], i, st)
                ln_transpose(pairs[0], i, 0, st)
            for i in range(8):
                if len(pairs) > 1:
                    ln_stats(pairs[1], i, st)
        for pi, pair in enumerate(pairs):
            cw0 = [("dve", rope_last.get(pi - 1, 0))] if rope_last.get(pi - 1, 0) > 0 else []
            sch.dma("sync", csAp[:, :], csA_d[:, pair * 1024:(pair + 1) * 1024], waits=cw0, inc="csld")
            sch.dma("sync", csBp[:, :], csB_d[:, pair * 1024:(pair + 1) * 1024], inc="csld")
            tabs = sch.cnt["csld"]
            if compute_ln:
                lnT_ready = st["done"][pair]
            else:
                lnT_ready = load_lnT(pair, mm_done.get(pi - 2, 0))
            lb = lnT[pair % 2]
            weave = compute_ln and (pi + 1 < len(pairs))
            for h in range(16):
                banks = [mm4[(h % 2) * 2], mm4[(h % 2) * 2 + 1]]
                for dc in range(16):
                    for half in range(2):
                        bank = banks[half]
                        w = []
                        if dc == 0:
                            w = [("lw", lw), lnT_ready]
                            if rope_done.get(id(bank), 0) > 0:
                                w.append(("dve", rope_done[id(bank)]))
                        sch.add("tensor", lambda e, bank=bank, dc=dc, h=h, half=half, lb=lb:
                                e.matmul(bank[:, :], wsb[:, dc * 2048 + h * 128: dc * 2048 + h * 128 + 128],
                                         lb[:, dc * 1024 + half * 512: dc * 1024 + (half + 1) * 512],
                                         start=(dc == 0), stop=(dc == 15), skip_group_check=True),
                                waits=w, inc=("pe" if (dc == 15 and half == 1) else None))
                for half in range(2):
                    col0 = pair * 1024 + half * 512
                    rope_evac(banks[half], h, pair, half, out_dram.ap()[h][:, col0:col0 + 512], stg_hist, rope_done, tabs)
                if weave:
                    if h < 8:
                        ln_transpose(pairs[pi + 1], h, mm_done.get(pi - 1, 0), st)
                    elif pi + 2 < len(pairs):
                        ln_stats(pairs[pi + 2], h - 8, st)
            mm_done[pi] = sch.cnt["pe"]
            rope_last[pi] = sch.cnt["dve"]

    if "k" in phases:
        sch.begin()
        load_w(wkT)
        sch.dma("sync", bcol[:, :], bkr[:, :], inc="csld")
        proj_dmajor(k_d, list(range(8)), True)
        sch.drain()
        sch.emit("p1k")

    if "q" in phases:
        sch.begin()
        load_w(wqT)
        sch.dma("sync", bcol[:, :], bqr[:, :], inc="csld")
        proj_dmajor(q_d, [0, 1], False)
        sch.drain()
        sch.emit("p1q")

    if "v" in phases:
        sch.begin()
        lwv = load_w(wvT)
        st_hist = 0
        vmm_done = {}
        for pair in range(8):
            lspill = load_lnT(pair, vmm_done.get(pair - 2, 0))
            lb = lnT[pair % 2]
            for ti in range(8):
                evd = sch.cnt["act"]
                for dc in range(16):
                    for dv in range(4):
                        w = []
                        if dc == 0 and dv == 0:
                            w = [("lw", lwv), lspill, ("act", evd)]
                        sch.add("tensor", lambda e, dc=dc, dv=dv, ti=ti, lb=lb:
                                e.matmul(mm4[dv][:, :],
                                         lb[:, dc * 1024 + ti * 128: dc * 1024 + ti * 128 + 128],
                                         wsb[:, dc * 2048 + dv * 512: dc * 2048 + (dv + 1) * 512],
                                         start=(dc == 0), stop=(dc == 15), skip_group_check=True),
                                waits=w, inc=("pe" if (dc == 15 and dv == 3) else None))
                pev = sch.cnt["pe"]
                chunk = pair * 8 + ti
                for dv in range(4):
                    w = [("pe", pev)]
                    if st_hist > 0:
                        w.append(("stv", st_hist))
                    sch.add("scalar", lambda e, dv=dv: e.activation(
                            lnbf[dv][:, 0:512], mm4[dv][:, :], AF.Copy),
                            waits=(w if dv == 0 else []), inc="act")
                evdone = sch.cnt["act"]
                for dv in range(4):
                    for qhh in range(4):
                        hh = dv * 4 + qhh
                        sch.dma("scalar", v_d.ap()[hh][:, chunk, :],
                                lnbf[dv][:, qhh * 128:(qhh + 1) * 128],
                                waits=[("act", evdone)], inc="stv")
                st_hist = sch.cnt["stv"]
            vmm_done[pair] = sch.cnt["pe"]
        sch.drain()
        sch.emit("p1v")

    p1s.close()

    catt = contextlib.ExitStack()
    C_all = catt.enter_context(nc.sbuf_tensor("C_all", [128, H * ROWS], BF16))
    if "att" in phases:
        with contextlib.ExitStack() as p2s:
            sb2 = lambda n, shp, dt: p2s.enter_context(nc.sbuf_tensor(n, shp, dt))
            ps2 = lambda n, shp, dt=F32: p2s.enter_context(nc.psum_tensor(n, shp, dt))
            kh = [sb2(f"p2_k{i}", [128, S], BF16) for i in range(2)]
            vh = [sb2(f"p2_v{i}", [128, S], BF16) for i in range(2)]
            qh = [sb2(f"p2_q{i}", [128, ROWS], BF16) for i in range(2)]
            pt_b = [sb2(f"p2_pt{i}", [128, 512], BF16) for i in range(3)]
            mk_sb = sb2("p2_masks", [128, 4 * 512], F32)
            sbsc = sb2("p2_sbsc", [128, NREST], F32)
            sbbi = sb2("p2_sbbi", [128, NREST], F32)
            rsum = sb2("p2_rsum", [1, 512], F32)
            rb = sb2("p2_rb", [128, 512], F32)
            ct = sb2("p2_ct", [128, 512], F32)
            bvc = sb2("p2_bvc", [128, 16], F32)
            psS = [ps2(f"p2_psS{i}", [128, 512]) for i in range(3)]
            psC = [ps2(f"p2_psC{i}", [128, 512]) for i in range(2)]
            psR = [ps2(f"p2_psR{i}", [128, 512]) for i in range(2)]
            psBC = ps2("p2_psBC", [128, 512])

            sch.begin()
            sch.dma("sync", mk_sb[:, :], masks_d[:, :], inc="dmc")
            sch.dma("sync", bvc[:, :], bvr_d[:, :], inc="dmc")
            sch.dma("sync", sbsc[:, :], sb_sc_d[:, :], inc="dmc")
            sch.dma("sync", sbbi[:, :], sb_bi_d[:, :], inc="dmc")
            tabs = sch.cnt["dmc"]
            kvq_done = {}
            cprev = {}
            ld = {}

            def load_head(h):
                i = h % 2
                hs = f"hd{i}"
                w = []
                if kvq_done.get(h - 2, 0) > 0:
                    w = [("pe", kvq_done[h - 2])]
                sch.dma("sync", kh[i][:, :], k_d.ap()[h], waits=w, inc=hs)
                sch.dma("sync", vh[i][:, :], v_d.ap()[h].rearrange("p c d -> p (c d)"), inc=hs)
                sch.dma("sync", qh[i][:, :], q_d.ap()[h], inc=hs)
                return (hs, sch.cnt[hs])

            ld[0] = load_head(0)
            ld[1] = load_head(1)
            for h in range(16):
                i = h % 2
                for s in range(4):
                    g = (h * 4 + s) % 2
                    bC = psC[g]
                    bR = psR[g]
                    units = slot_units(s)
                    nu = len(units)
                    exp_done = {}
                    sv = {}

                    def emit_S(u):
                        kind, kt, aux = units[u]
                        bS = psS[u % 3]
                        if u < 3:
                            w = [ld[h], ("act", sch.cnt["act"])]
                        else:
                            w = [("act", exp_done[u - 3])]
                        sch.add("tensor", lambda e, bS=bS, kt=kt, i=i, s=s:
                                e.matmul(bS[:, :], kh[i][:, kt * 128:(kt + 1) * 128],
                                         qh[i][:, s * 512:(s + 1) * 512], start=True, stop=True,
                                         skip_group_check=True),
                                waits=w, inc="pe")
                        sv[u] = sch.cnt["pe"]

                    for uu in range(min(3, nu)):
                        emit_S(uu)
                    for u, (kind, kt, aux) in enumerate(units):
                        bS = psS[u % 3]
                        if kind == "diag":
                            sch.add("vector", lambda e, bS=bS, aux=aux: e.tensor_tensor(
                                    bS[:, :], bS[:, :], mk_sb[:, aux * 512:(aux + 1) * 512], ALU.add),
                                    waits=[("pe", sv[u]), ("dmc", tabs)], inc="dve")
                            mw = [("dve", sch.cnt["dve"])]
                        else:
                            mw = [("pe", sv[u])]
                        ptb = pt_b[u % 3]
                        if kind == "rest":
                            sch.add("scalar", lambda e, bS=bS, ptb=ptb, aux=aux:
                                    e.activation(ptb[:, :], bS[:, :], AF.Exp,
                                                 bias=sbbi[:, aux:aux + 1], scale=sbsc[:, aux:aux + 1]),
                                    waits=mw + [("dmc", tabs)], inc="act")
                        else:
                            sch.add("scalar", lambda e, bS=bS, ptb=ptb:
                                    e.activation(ptb[:, :], bS[:, :], AF.Exp, bias=cst[:, 1:2], scale=cst[:, 2:3]),
                                    waits=mw, inc="act")
                        exp_done[u] = sch.cnt["act"]
                        wpv = [("act", exp_done[u])]
                        if u == 0 and cprev.get(g, 0) > 0:
                            wpv.append(("dve", cprev[g]))
                        sch.add("tensor", lambda e, ptb=ptb, kt=kt, i=i, bC=bC, first=(u == 0), last=(u == nu - 1):
                                e.matmul(bC[:, :], vh[i][:, kt * 128:(kt + 1) * 128], ptb[:, :],
                                         start=first, stop=last, skip_group_check=True),
                                waits=wpv, inc=None)
                        sch.add("tensor", lambda e, ptb=ptb, bR=bR, first=(u == 0), last=(u == nu - 1):
                                e.matmul(bR[0:1, :], ones_bf[:, :], ptb[:, :],
                                         start=first, stop=last, skip_group_check=True),
                                inc="pe")
                        if u + 3 < nu:
                            emit_S(u + 3)
                    pv_done = sch.cnt["pe"]
                    sch.add("vector", lambda e, bR=bR: e.reciprocal(rsum[0:1, :], bR[0:1, :]),
                            waits=[("pe", pv_done)], inc="dve")
                    sch.add("tensor", lambda e: e.matmul(psBC[:, :], ones1f[0:1, :], rsum[0:1, :],
                                                         start=True, stop=True, skip_group_check=True),
                            waits=["dve"], inc="pe")
                    sch.add("vector", lambda e: e.tensor_copy(rb[:, :], psBC[:, :]),
                            waits=["pe"], inc="dve")
                    sch.add("vector", lambda e, bC=bC: e.tensor_tensor(
                            ct[:, :], bC[:, :], rb[:, :], ALU.mult),
                            waits=["dve"], inc="dve")
                    sch.add("vector", lambda e, h=h, s=s: e.tensor_scalar(
                            C_all[:, h * ROWS + s * 512: h * ROWS + (s + 1) * 512], ct[:, :], bvc[:, h:h + 1], None, ALU.add),
                            waits=["dve"], inc="dve")
                    cprev[g] = sch.cnt["dve"]
                kvq_done[h] = sch.cnt["pe"]
                if h + 2 < 16:
                    ld[h + 2] = load_head(h + 2)
            if dbg is not None and dbg[0] == "C_all":
                sch.dma("gpsimd", c_d.ap(), C_all.ap(), waits=[("dve", sch.cnt["dve"])], inc="dmg")
            sch.drain()
            sch.emit("p2")

    if "oproj" in phases:
        with contextlib.ExitStack() as p3s:
            sb3 = lambda n, shp, dt: p3s.enter_context(nc.sbuf_tensor(n, shp, dt))
            ps3 = lambda n, shp, dt=F32: p3s.enter_context(nc.psum_tensor(n, shp, dt))
            wosb = sb3("p3_wo", [128, 16 * 2048], BF16)
            ident_f = sb3("p3_identf", [128, 128], F32)
            xb3 = sb3("p3_x", [128, 2 * 2048], F32)
            xp3 = sb3("p3_xp", [128, 2048], F32)
            st3 = sb3("p3_stat", [128, 16], F32)
            ln3f_a = sb3("p3_lnfa", [128, 2048], F32)
            ln3f_b = sb3("p3_lnfb", [128, 2048], F32)
            tp3 = sb3("p3_tp", [128, 32], F32)
            lnTs = sb3("p3_lnTs", [128, 16 * 128], F32)
            lnTq = sb3("p3_lnTq", [128, 16 * 512], BF16)
            bob = sb3("p3_bob", [128, D], F32)
            gsb = sb3("p3_g", [128, 128], F32)
            psO = [ps3(f"p3_psO{i}", [128, 512]) for i in range(4)]
            psT3 = [ps3(f"p3_psT{i}", [128, 128]) for i in range(2)]
            psG = ps3("p3_psG", [128, 8])

            sch.begin()
            for dc in range(16):
                sch.dma("gpsimd", wosb[:, dc * 2048:(dc + 1) * 2048],
                        woT[dc * 128:(dc + 1) * 128, :], inc="lw")
            lwo = sch.cnt["lw"]
            sch.dma("sync", bob[:, :], bo_bc_d[:, :], inc="dmc")
            sch.dma("sync", gsb[:, :], gate_r[:, :], inc="dmc")
            sch.dma("sync", ident_f[:, :], ident_d[:, :], inc="dmc")
            tb3 = sch.cnt["dmc"]
            xread, xpst, lnq_st, tev = {}, {}, 0, {}
            ln3f = [ln3f_a, ln3f_b]
            lnfd, gstate = {}, {}
            for tt in range(17):
                if tt < 16:
                    sl = tt % 2
                    xslot = xb3[:, sl * 2048:(sl + 1) * 2048]
                    dxs = f"dx{sl}"
                    w = []
                    if xread.get(tt - 2, 0) > 0:
                        w = [("dve", xread[tt - 2])]
                    sch.dma("sync", xslot, x[tt * 128:(tt + 1) * 128, :], waits=w, inc=dxs)
                    ldx = (dxs, sch.cnt[dxs])
                    evd = sch.cnt["dve"]
                    for ic in range(16):
                        for oc in range(4):
                            w = []
                            if ic == 0 and oc == 0:
                                w = [("lw", lwo), ("dve", evd)]
                            sch.add("tensor", lambda e, ic=ic, oc=oc, tt=tt:
                                    e.matmul(psO[oc][:, :],
                                             C_all[:, ic * ROWS + tt * 128: ic * ROWS + (tt + 1) * 128],
                                             wosb[:, ic * 2048 + oc * 512: ic * 2048 + (oc + 1) * 512],
                                             start=(ic == 0), stop=(ic == 15), skip_group_check=True),
                                    waits=w, inc=("pe" if (ic == 15 and oc == 3) else None))
                    odone = sch.cnt["pe"]
                    for oc in range(4):
                        w = [("pe", odone), ldx, ("dmc", tb3)]
                        if xpst.get(tt - 1, 0) > 0:
                            w.append(("stx", xpst[tt - 1]))
                        sch.add("vector", lambda e, oc=oc, xslot=xslot: e.tensor_tensor(
                                xp3[:, oc * 512:(oc + 1) * 512], psO[oc][:, :], xslot[:, oc * 512:(oc + 1) * 512], ALU.add),
                                waits=(w if oc == 0 else []), inc="dve")
                        sch.add("vector", lambda e, oc=oc: e.tensor_tensor(
                                xp3[:, oc * 512:(oc + 1) * 512], xp3[:, oc * 512:(oc + 1) * 512], bob[:, oc * 512:(oc + 1) * 512], ALU.add),
                                waits=["dve"], inc="dve")
                    xread[tt] = sch.cnt["dve"]
                    xr = sch.cnt["dve"]
                    sch.dma("scalar", xp_d[tt * 128:(tt + 1) * 128, :], xp3[:, :], waits=[("dve", xr)], inc="stx")
                    xpst[tt] = sch.cnt["stx"]
                    lf = ln3f[tt % 2]
                    sch.add("scalar", lambda e, lf=lf: e.activation(lf[:, :], xp3[:, :], AF.Copy, accum_out=st3[:, 0:1]),
                            waits=[("dve", xr), ("dve", tev.get(tt - 1, 0)), ("pe", sch.cnt["pe"])], inc="act")
                    sch.add("scalar", lambda e, lf=lf: e.activation(lf[:, :], xp3[:, :], AF.Square, accum_out=st3[:, 1:2]),
                            waits=[], inc="act")
                    sch.add("vector", lambda e: e.tensor_scalar(st3[:, 2:3], st3[:, 0:1], 1.0 / 2048.0, None, ALU.mult), waits=["act"], inc="dve")
                    sch.add("vector", lambda e: e.tensor_tensor(st3[:, 3:4], st3[:, 2:3], st3[:, 2:3], ALU.mult), waits=["dve"], inc="dve")
                    sch.add("vector", lambda e: e.tensor_scalar(st3[:, 4:5], st3[:, 1:2], 1.0 / 2048.0, None, ALU.mult), waits=["dve"], inc="dve")
                    sch.add("vector", lambda e: e.tensor_tensor(st3[:, 5:6], st3[:, 4:5], st3[:, 3:4], ALU.subtract), waits=["dve"], inc="dve")
                    sch.add("scalar", lambda e: e.activation(st3[:, 6:7], st3[:, 5:6], AF.Sqrt, bias=cst[:, 0:1]), waits=["dve"], inc="act")
                    sch.add("vector", lambda e: e.reciprocal(st3[:, 7:8], st3[:, 6:7]), waits=["act"], inc="dve")
                    sch.add("vector", lambda e: e.scalar_tensor_tensor(st3[:, 8:9], st3[:, 2:3], -1.0, st3[:, 7:8], ALU.mult, ALU.mult), waits=["dve"], inc="dve")
                    sch.add("vector", lambda e, lf=lf: e.tensor_scalar(lf[:, :], xp3[:, :], st3[:, 7:8], st3[:, 8:9], ALU.mult, ALU.add), waits=["dve"], inc="dve")
                    lnfd[tt] = ("dve", sch.cnt["dve"])
                pt_ = tt - 1
                if pt_ < 0:
                    continue
                lf = ln3f[pt_ % 2]
                for dc in range(16):
                    bk = dc % 2
                    w = [lnfd[pt_]]
                    if tev.get(("b", bk), 0) > 0:
                        w.append(("dve", tev[("b", bk)]))
                    if tev.get(("a", bk), 0) > 0:
                        w.append(("act", tev[("a", bk)]))
                    sch.add("tensor", lambda e, bk=bk, dc=dc, lf=lf: e.transpose(psT3[bk][:, :], lf[:, dc * 128:(dc + 1) * 128], ident_f[:, :]),
                            waits=w, inc="pe")
                    pev = sch.cnt["pe"]
                    sch.add("scalar", lambda e, bk=bk, dc=dc: e.activation(lnTs[:, dc * 128:(dc + 1) * 128], psT3[bk][:, :], AF.Copy),
                            waits=[("pe", pev)], inc="act")
                    tev[("a", bk)] = sch.cnt["act"]
                    w = [("act", sch.cnt["act"])]
                    if lnq_st > 0:
                        w.append(("stln", lnq_st))
                    sch.add("vector", lambda e, dc=dc, pt_=pt_: e.tensor_copy(
                            lnTq[:, dc * 512 + (pt_ % 4) * 128: dc * 512 + (pt_ % 4) * 128 + 128],
                            lnTs[:, dc * 128:(dc + 1) * 128]),
                            waits=w, inc="dve")
                    tev[("b", bk)] = sch.cnt["dve"]
                tev[pt_] = sch.cnt["dve"]
                tdone_act = sch.cnt["act"]
                for dc in range(16):
                    sch.add("tensor", lambda e, dc=dc: e.matmul(psG[:, :], lnTs[:, dc * 128:(dc + 1) * 128],
                                                                gsb[:, dc * 8:(dc + 1) * 8],
                                                                start=(dc == 0), stop=(dc == 15), skip_group_check=True),
                            waits=([("act", tdone_act), ("dve", sch.cnt["dve"])] if dc == 0 else []),
                            inc=("pe" if dc == 15 else None))
                gd = sch.cnt["pe"]
                sch.add("vector", lambda e: e.reduce_max(st3[:, 9:10], psG[:, :], axis=AX.X), waits=[("pe", gd)], inc="dve")
                sch.add("vector", lambda e: e.tensor_scalar(tp3[:, 0:8], psG[:, :], st3[:, 9:10], None, ALU.is_ge), waits=["dve"], inc="dve")
                sch.add("vector", lambda e: e.scalar_tensor_tensor(tp3[:, 8:16], tp3[:, 0:8], cst[:, 3:4], psG[:, :], ALU.mult, ALU.add), waits=["dve"], inc="dve")
                sch.add("vector", lambda e: e.reduce_max(st3[:, 10:11], tp3[:, 8:16], axis=AX.X), waits=["dve"], inc="dve")
                sch.add("vector", lambda e: e.tensor_scalar(tp3[:, 16:24], tp3[:, 8:16], st3[:, 10:11], None, ALU.is_ge), waits=["dve"], inc="dve")
                sch.add("vector", lambda e: e.tensor_tensor(st3[:, 11:12], st3[:, 10:11], st3[:, 9:10], ALU.subtract), waits=["dve"], inc="dve")
                sch.add("scalar", lambda e: e.activation(st3[:, 12:13], st3[:, 11:12], AF.Sigmoid), waits=["dve"], inc="act")
                sch.add("vector", lambda e: e.tensor_scalar(st3[:, 13:14], st3[:, 12:13], -1.0, 1.0, ALU.mult, ALU.add), waits=["act"], inc="dve")
                sch.add("vector", lambda e: e.tensor_scalar(tp3[:, 24:32], tp3[:, 0:8], st3[:, 13:14], None, ALU.mult), waits=["dve"], inc="dve")
                sch.add("vector", lambda e, pt_=pt_: e.scalar_tensor_tensor(
                        cw_sb[:, pt_ * 8:(pt_ + 1) * 8], tp3[:, 16:24], st3[:, 12:13], tp3[:, 24:32], ALU.mult, ALU.add),
                        waits=["dve"], inc="dve")
                if pt_ % 4 == 3:
                    qt = pt_ // 4
                    src = lnTq.ap().rearrange("p (c t) -> p c t", t=512)
                    dst = ln2T_d.ap()[qt].rearrange("(c p) t -> p c t", p=128)
                    sch.dma("scalar", dst, src, waits=[("dve", sch.cnt["dve"])], inc="stln")
                    lnq_st = sch.cnt["stln"]
            sch.drain()
            sch.emit("p3")
    catt.close()

    if "ffn" in phases:
        with contextlib.ExitStack() as p4s:
            sb4 = lambda n, shp, dt: p4s.enter_context(nc.sbuf_tensor(n, shp, dt))
            ps4 = lambda n, shp, dt=F32: p4s.enter_context(nc.psum_tensor(n, shp, dt))
            ln2h = sb4("p4_ln2", [128, 16 * 1024], BF16)
            wbuf = sb4("p4_w", [128, 32 * 1024], BF16)
            hbuf = sb4("p4_h", [128, 16 * 1024], BF16)
            acc = sb4("p4_acc", [128, 8 * 2048], F32)
            silu_t = sb4("p4_silu", [128, 512], BF16)
            psF = [ps4(f"p4_psF{i}", [128, 512]) for i in range(8)]

            sch.begin()
            for half in range(2):
                for qi in range(2):
                    qt = half * 2 + qi
                    dst = ln2h.ap().rearrange("p (c t) -> p c t", t=1024)[:, :, qi * 512:(qi + 1) * 512]
                    src = ln2T_d.ap()[qt].rearrange("(c p) t -> p c t", p=128)
                    sch.dma("sync", dst, src, waits=[("pe", sch.cnt["pe"])], inc="spill")
                for t8 in range(8):
                    row = half * 1024 + t8 * 128
                    sch.dma("sync", acc[:, t8 * 2048:(t8 + 1) * 2048], xp_d[row:row + 128, :],
                            waits=[("sto", sch.cnt["sto"]), ("dve", sch.cnt["dve"])], inc="spill")
                base = sch.cnt["spill"]

                ring_free = {-2: sch.cnt["pe"], -1: sch.cnt["pe"]}

                def load_wi(it, wdram, e_, mh):
                    slot = it % 2
                    wrs = f"wr{slot}"
                    w = [("pe", ring_free[it - 2])] if ring_free.get(it - 2, 0) > 0 else []
                    for dc in range(16):
                        sch.dma("gpsimd", wbuf[:, slot * 16384 + dc * 1024: slot * 16384 + (dc + 1) * 1024],
                                wdram.ap()[e_][dc * 128:(dc + 1) * 128, mh * 1024:(mh + 1) * 1024],
                                waits=(w if dc == 0 else []), inc=wrs)
                    return (wrs, sch.cnt[wrs])

                def load_wo(it, e_, mh):
                    slot = it % 2
                    wrs = f"wr{slot}"
                    w = [("pe", ring_free[it - 2])] if ring_free.get(it - 2, 0) > 0 else []
                    for mD in range(8):
                        m0 = mh * 8 + mD
                        sch.dma("gpsimd", wbuf[:, slot * 16384 + mD * 2048: slot * 16384 + (mD + 1) * 2048],
                                wof.ap()[e_][m0 * 128:(m0 + 1) * 128, :],
                                waits=(w if mD == 0 else []), inc=wrs)
                    return (wrs, sch.cnt[wrs])

                plan = []
                for e_ in range(E):
                    plan += [("a0", e_, 0), ("a0", e_, 1), ("a1", e_, 0), ("a1", e_, 1),
                             ("b", e_, 0), ("b", e_, 1)]
                for it, (kindp, e_, mh) in enumerate(plan):
                    if kindp == "a0":
                        lw_t = load_wi(it, wi0, e_, mh)
                    elif kindp == "a1":
                        lw_t = load_wi(it, wi1, e_, mh)
                    else:
                        lw_t = load_wo(it, e_, mh)
                    slot = it % 2
                    if kindp in ("a0", "a1"):
                        hev = {}
                        for mc in range(8):
                            m_abs = mh * 8 + mc
                            bks = [psF[(mc % 4) * 2], psF[(mc % 4) * 2 + 1]]
                            w = [lw_t, ("spill", base)]
                            if mc == 0:
                                w += [("act", sch.cnt["act"]), ("dve", sch.cnt["dve"])]
                            if mc >= 4:
                                w.append(("act" if kindp == "a0" else "dve",
                                          hev.get((mc % 4), 0)))
                            for dc in range(16):
                                for tc in range(2):
                                    sch.add("tensor", lambda e, bank=bks[tc], dc=dc, mc=mc, tc=tc, slot=slot:
                                            e.matmul(bank[:, :],
                                                     wbuf[:, slot * 16384 + dc * 1024 + mc * 128: slot * 16384 + dc * 1024 + mc * 128 + 128],
                                                     ln2h[:, dc * 1024 + tc * 512: dc * 1024 + (tc + 1) * 512],
                                                     start=(dc == 0), stop=(dc == 15), skip_group_check=True),
                                            waits=(w if (dc == 0 and tc == 0) else []),
                                            inc=("pe" if (dc == 15 and tc == 1) else None))
                            hd = sch.cnt["pe"]
                            for tc in range(2):
                                col = m_abs * 1024 + tc * 512
                                if kindp == "a0":
                                    sch.add("scalar", lambda e, bank=bks[tc], col=col: e.activation(
                                            hbuf[:, col:col + 512], bank[:, :], AF.Copy),
                                            waits=[("pe", hd)], inc="act")
                                    hev[mc % 4] = sch.cnt["act"]
                                else:
                                    sch.add("scalar", lambda e, col=col: e.activation(
                                            silu_t[:, :], hbuf[:, col:col + 512], AF.Silu),
                                            waits=[("pe", hd), ("dve", sch.cnt["dve"])], inc="act")
                                    sch.add("vector", lambda e, bank=bks[tc], col=col: e.tensor_tensor(
                                            hbuf[:, col:col + 512], silu_t[:, :], bank[:, :], ALU.mult),
                                            waits=["act"], inc="dve")
                                    hev[mc % 4] = sch.cnt["dve"]
                        ring_free[it] = sch.cnt["pe"]
                    else:
                        bev = {}
                        for t8 in range(8):
                            w0 = [lw_t, ("dve", bev.get(t8 % 2, sch.cnt["dve"]))]
                            if t8 == 0:
                                w0.append(("dve", sch.cnt["dve"]))
                            bset = [psF[(t8 % 2) * 4 + k] for k in range(4)]
                            for mD in range(8):
                                for oc in range(4):
                                    sch.add("tensor", lambda e, t8=t8, mD=mD, oc=oc, slot=slot, mh=mh, bset=bset:
                                            e.matmul(bset[oc][:, :],
                                                     hbuf[:, (mh * 8 + mD) * 1024 + t8 * 128: (mh * 8 + mD) * 1024 + t8 * 128 + 128],
                                                     wbuf[:, slot * 16384 + mD * 2048 + oc * 512: slot * 16384 + mD * 2048 + (oc + 1) * 512],
                                                     start=(mD == 0), stop=(mD == 7), skip_group_check=True),
                                            waits=(w0 if (mD == 0 and oc == 0) else []),
                                            inc=("pe" if (mD == 7 and oc == 3) else None))
                            bd = sch.cnt["pe"]
                            for oc in range(4):
                                sch.add("vector", lambda e, t8=t8, oc=oc, e_=e_, half=half, bset=bset: e.scalar_tensor_tensor(
                                        acc[:, t8 * 2048 + oc * 512: t8 * 2048 + (oc + 1) * 512],
                                        bset[oc][:, :],
                                        cw_sb[:, (half * 8 + t8) * 8 + e_:(half * 8 + t8) * 8 + e_ + 1],
                                        acc[:, t8 * 2048 + oc * 512: t8 * 2048 + (oc + 1) * 512],
                                        ALU.mult, ALU.add),
                                        waits=([("pe", bd)] if oc == 0 else []), inc="dve")
                            bev[t8 % 2] = sch.cnt["dve"]
                        ring_free[it] = sch.cnt["pe"]
                accd = sch.cnt["dve"]
                for t8 in range(8):
                    row = half * 1024 + t8 * 128
                    sch.dma("scalar", out[row:row + 128, :], acc[:, t8 * 2048:(t8 + 1) * 2048],
                            waits=[("dve", accd)], inc="sto")
            sch.drain()
            sch.emit("p4")

    if dbg is not None and dbg[0] == "cw":
        sch.begin()
        sch.dma("gpsimd", cw_d.ap(), cw_sb.ap(), inc="dmg")
        sch.dma("gpsimd", dbg_out.ap(), cw_d.ap(), waits=[("dmg", sch.cnt["dmg"])], inc="dmg")
        sch.wait_only("gpsimd", [("dmg", sch.cnt["dmg"])])
        sch.emit("dbgblk")
        dbg = None
    if dbg is not None:
        name = dbg[0]
        srcs = {"ln1T_d": ln1T_d, "q_d": q_d, "k_d": k_d, "v_d": v_d,
                "ln2T_d": ln2T_d, "xp_d": xp_d, "C_all": c_d}
        src = srcs[name]
        sch.begin()
        sch.dma("gpsimd", dbg_out.ap(), src.ap(), inc="dmg")
        sch.wait_only("gpsimd", [("dmg", sch.cnt["dmg"])])
        sch.emit("dbgblk")

    return nc



def _rope_tables():
    inv = (1.0 / (10000.0 ** (np.arange(0, DH, 2, dtype=np.float32) / np.float32(DH)))).astype(np.float32)
    t = np.arange(S, dtype=np.float32)
    fr = t[None, :] * inv[:, None]
    return np.cos(fr).astype(np.float32), np.sin(fr).astype(np.float32)


def core_perm(core):
    j = core % 4
    chunks = own_chunks(j)
    own = np.concatenate([np.arange(c * 512, (c + 1) * 512) for c in chunks])
    rest = np.setdiff1d(np.arange(S), own)
    return np.concatenate([own, rest]), chunks


def make_inputs(core, full):
    b = core // 4
    perm, chunks = core_perm(core)

    cos, sin = _rope_tables()
    csA = np.concatenate([cos[:, perm], cos[:, perm]], axis=0)
    csB = np.concatenate([sin[:, perm], -sin[:, perm]], axis=0)

    n_s = [4 * (chunks[s] - s) for s in range(4)]
    sc = np.zeros(NREST, np.float32)
    bi = np.zeros(NREST, np.float32)
    for s in range(4):
        for r in range(NMAX[s]):
            u = CUM[s] + r
            if r < n_s[s]:
                sc[u] = SCALE
                bi[u] = -CEXP
            else:
                sc[u] = 0.0
                bi[u] = -100.0
    sb_scale = np.ascontiguousarray(np.broadcast_to(sc, (128, NREST)))
    sb_bias = np.ascontiguousarray(np.broadcast_to(bi, (128, NREST)))

    mk = np.zeros((128, 4, 512), np.float32)
    kk = np.arange(128)[:, None]
    qq = np.arange(512)[None, :]
    for dk in range(4):
        mk[:, dk, :] = np.where(qq >= dk * 128 + kk, 0.0, -1e9)
    masks = np.ascontiguousarray(mk.reshape(128, 4 * 512))

    d = {
        "x": np.ascontiguousarray(full["L0_attn_hidden_states"][b][perm]),
        "csA": np.ascontiguousarray(csA),
        "csB": np.ascontiguousarray(csB),
        "wqT": np.ascontiguousarray(full["L0_attn_Wq"].T),
        "wkT": np.ascontiguousarray(full["L0_attn_Wk"].T),
        "wvT": np.ascontiguousarray(full["L0_attn_Wv"].T),
        "woT": np.ascontiguousarray(full["L0_attn_Wo"].T),
        "bqr": np.ascontiguousarray(full["L0_attn_bq"].reshape(16, 128).T),
        "bkr": np.ascontiguousarray(full["L0_attn_bk"].reshape(16, 128).T),
        "bvr": np.ascontiguousarray(full["L0_attn_bv"].reshape(16, 128).T),
        "bo_bc": np.ascontiguousarray(np.broadcast_to(full["L0_attn_bo"], (128, D))),
        "gate_r": np.ascontiguousarray(
            full["L0_ffn_gate_kernel"].reshape(16, 128, 8).transpose(1, 0, 2).reshape(128, 128)),
        "wi0": np.ascontiguousarray(full["L0_ffn_wi_0"]),
        "wi1": np.ascontiguousarray(full["L0_ffn_wi_1"]),
        "wof": np.ascontiguousarray(full["L0_ffn_wo"]),
        "sb_scale": sb_scale,
        "sb_bias": sb_bias,
        "masks": masks,
        "ident": np.eye(128, dtype=np.float32),
        "ones_in": np.ones((2, 128), np.float32),
        "constv": np.ascontiguousarray(np.broadcast_to(
            np.array([EPS, -CEXP, SCALE, -1e9, 1.0 / 2048.0, 0, 0, 0], np.float32), (128, 8))),
    }
    return d


def kernel(**inputs):
    full = {k: np.asarray(v) for k, v in inputs.items()}
    nc = build_program()
    in_maps = [make_inputs(core, full) for core in range(8)]
    res = run_bass_kernel_spmd(nc, in_maps, core_ids=list(range(8)))
    outp = np.zeros((B, S, D), np.float32)
    for core in range(8):
        b = core // 4
        perm, _ = core_perm(core)
        outp[b][perm[:ROWS]] = res.results[core]["out"]
    return outp


# version 10
# speedup vs baseline: 1.0248x; 1.0248x over previous
import contextlib
import numpy as np
import concourse.bass as bass
import concourse.mybir as mybir
from concourse.bass_utils import run_bass_kernel_spmd

F32 = mybir.dt.float32
BF16 = mybir.dt.bfloat16
AF = mybir.ActivationFunctionType
ALU = mybir.AluOpType
AX = mybir.AxisListType

B, S, D, H, DH, E, M = 2, 8192, 2048, 16, 128, 8, 2048
ROWS = 2048
NQT = S // 512
SCALE = float(np.float32(1.0) / np.float32(np.sqrt(np.float32(128.0))))
CEXP = 12.0
NMAX = [12, 24, 36, 48]
CUM = [0, 12, 36, 72]
NREST = 120
EPS = 1e-5
ENGS = ("sync", "gpsimd", "scalar", "vector", "tensor")
P3_LEVEL = 4

DMA_SEMS = ("dmc", "dmg", "dx0", "dx1", "lw", "wr0", "wr1", "spill", "sp0", "sp1", "csld",
            "hd0", "hd1", "st0", "st1", "st2", "st3", "stv", "stln", "stx", "sto",
            "xb0", "xb1")


def own_chunks(j):
    return sorted({j, 7 - j, 8 + j, 15 - j})


def slot_units(s):
    units = []
    for t in range(4 * s):
        units.append(("own", t, None))
    for dk in range(4):
        units.append(("diag", 4 * s + dk, dk))
    for r in range(NMAX[s]):
        units.append(("rest", 16 + r, CUM[s] + r))
    return units


class Sch:

    def __init__(self, nc):
        self.nc = nc
        self.stack = contextlib.ExitStack()
        self.sem = {}
        self.cnt = {}
        self.q = None

    def sems(self, *names):
        for n in names:
            self.sem[n] = self.stack.enter_context(self.nc.semaphore(n))
            self.cnt[n] = 0

    def begin(self):
        assert self.q is None
        self.q = {e: [] for e in ENGS}

    def add(self, eng, fn, waits=(), inc=None, by=1, auto=True):
        wd = {}
        for it in waits:
            if isinstance(it, tuple):
                nm, val = it
            else:
                nm, val = it, self.cnt[it]
            wd[nm] = max(wd.get(nm, 0), val)
        if auto and eng == "vector":
            wd["dve"] = max(wd.get("dve", 0), self.cnt["dve"])
        if auto and eng == "scalar":
            wd["act"] = max(wd.get("act", 0), self.cnt["act"])
        w = [(nm, val) for nm, val in wd.items() if val > 0]
        if inc is not None:
            self.cnt[inc] += by
        self.q[eng].append((fn, w, inc, by))

    def dma(self, eng, out, in_, waits=(), inc=None, by=16, **kw):
        assert inc is not None
        self.add(eng, lambda e: e.dma_start(out, in_, **kw), waits, inc, by, auto=False)

    def wait_only(self, eng, waits):
        self.add(eng, lambda e: None, waits, None)

    def drain(self):
        self.wait_only("gpsimd", [(n, self.cnt[n]) for n in DMA_SEMS])

    def emit(self, name):
        q, self.q = self.q, None
        sem = self.sem

        def mk(eng_name):
            ops = q[eng_name]

            def body(e):
                for fn, w, inc, by in ops:
                    for sname, val in w:
                        e.wait_ge(sem[sname], val)
                    ins = fn(e)
                    if inc is not None:
                        assert ins is not None
                        ins.then_inc(sem[inc], by)
            return body
        with self.nc.Block(name) as blk:
            if q["gpsimd"]:
                blk.gpsimd(mk("gpsimd"))
            if q["scalar"]:
                blk.scalar(mk("scalar"))
            if q["tensor"]:
                blk.tensor(mk("tensor"))
            if q["vector"]:
                blk.vector(mk("vector"))
            if q["sync"]:
                blk.sync(mk("sync"))


def build_program(phases=("k", "q", "v", "att", "oproj", "ffn"), dbg=None):
    nc = bass.Bass("TRN2", target_bir_lowering=False)
    sch = Sch(nc)
    sb = lambda name, shape, dt: sch.stack.enter_context(nc.sbuf_tensor(name, shape, dt))

    x = nc.dram_tensor("x", [S, D], F32, kind="ExternalInput")
    csA_d = nc.dram_tensor("csA", [128, S], F32, kind="ExternalInput")
    csB_d = nc.dram_tensor("csB", [128, S], F32, kind="ExternalInput")
    wqT = nc.dram_tensor("wqT", [D, D], F32, kind="ExternalInput")
    wkT = nc.dram_tensor("wkT", [D, D], F32, kind="ExternalInput")
    wvT = nc.dram_tensor("wvT", [D, D], F32, kind="ExternalInput")
    woT = nc.dram_tensor("woT", [D, D], F32, kind="ExternalInput")
    bqr = nc.dram_tensor("bqr", [128, 16], F32, kind="ExternalInput")
    bkr = nc.dram_tensor("bkr", [128, 16], F32, kind="ExternalInput")
    bvr_d = nc.dram_tensor("bvr", [128, 16], F32, kind="ExternalInput")
    bo_bc_d = nc.dram_tensor("bo_bc", [128, D], F32, kind="ExternalInput")
    gate_r = nc.dram_tensor("gate_r", [128, 128], F32, kind="ExternalInput")
    wi0 = nc.dram_tensor("wi0", [E, D, M], F32, kind="ExternalInput")
    wi1 = nc.dram_tensor("wi1", [E, D, M], F32, kind="ExternalInput")
    wof = nc.dram_tensor("wof", [E, M, D], F32, kind="ExternalInput")
    sb_sc_d = nc.dram_tensor("sb_scale", [128, NREST], F32, kind="ExternalInput")
    sb_bi_d = nc.dram_tensor("sb_bias", [128, NREST], F32, kind="ExternalInput")
    masks_d = nc.dram_tensor("masks", [128, 4 * 512], F32, kind="ExternalInput")
    ident_d = nc.dram_tensor("ident", [128, 128], F32, kind="ExternalInput")
    ones_d = nc.dram_tensor("ones_in", [2, 128], F32, kind="ExternalInput")
    constv = nc.dram_tensor("constv", [128, 8], F32, kind="ExternalInput")
    out = nc.dram_tensor("out", [ROWS, D], F32, kind="ExternalOutput")
    dbg_out = None
    if dbg is not None:
        dbg_out = nc.dram_tensor("dbg", list(dbg[1]), dbg[2], kind="ExternalOutput")

    ln1T_d = nc.dram_tensor("ln1T_d", [NQT, D, 512], BF16, kind="Internal")
    q_d = nc.dram_tensor("q_d", [H, 128, ROWS], BF16, kind="Internal")
    k_d = nc.dram_tensor("k_d", [H, 128, S], BF16, kind="Internal")
    v_d = nc.dram_tensor("v_d", [H, 128, 64, 128], BF16, kind="Internal")
    ln2T_d = nc.dram_tensor("ln2T_d", [4, D, 512], BF16, kind="Internal")
    xp_d = nc.dram_tensor("xp_d", [ROWS, D], F32, kind="Internal")
    c_d = nc.dram_tensor("c_d", [128, H * ROWS], BF16, kind="Internal")
    cw_d = nc.dram_tensor("cw_d", [128, 128], F32, kind="Internal")

    sch.sems(*DMA_SEMS)
    sch.sems("pe", "act", "dve")

    ident_bf = sb("ident_bf", [128, 128], BF16)
    ones_bf = sb("ones_bf", [128, 1], BF16)
    ones1f = sb("ones1f", [1, 128], F32)
    cw_sb = sb("cw_sb", [128, 16 * 8], F32)
    cst = sb("cst", [128, 8], F32)

    sch.begin()
    sch.dma("gpsimd", ident_bf[:, :], ident_d[:, :], inc="dmg")
    sch.dma("gpsimd", ones_bf[:, :], ones_d[0:1, :].rearrange("c p -> p c"), inc="dmg",
            allow_slow_non_contiguous=True)
    sch.dma("sync", ones1f[0:1, :], ones_d[1:2, :], inc="dmc")
    sch.dma("sync", cst[:, :], constv[:, :], inc="dmc")
    _CONSTS_PENDING = True

    p1s = contextlib.ExitStack()
    sb1 = lambda n, shp, dt: p1s.enter_context(nc.sbuf_tensor(n, shp, dt))
    ps1 = lambda n, shp, dt=F32: p1s.enter_context(nc.psum_tensor(n, shp, dt))
    wsb = sb1("p1_w", [128, 16 * 2048], BF16)
    lnT = [sb1(f"p1_lnT{i}", [128, 16 * 1024], BF16) for i in range(2)]
    xb = sb1("p1_x", [128, 2 * 2048], F32)
    lnbf = [sb1(f"p1_lnbf{i}", [128, 2048], BF16) for i in range(8)]
    scr = sb1("p1_scr", [128, 2048], F32)
    stat = sb1("p1_stat", [128, 16], F32)
    csAp = sb1("p1_csAp", [128, 1024], F32)
    csBp = sb1("p1_csBp", [128, 1024], F32)
    t1 = sb1("p1_t1", [128, 512], F32)
    t2 = sb1("p1_t2", [128, 512], F32)
    t3 = sb1("p1_t3", [128, 512], F32)
    stg = sb1("p1_stg", [128, 2 * 512], BF16)
    bcol = sb1("p1_b", [128, 16], F32)
    bvb = scr
    mm4 = [ps1(f"p1_mm{i}", [128, 512]) for i in range(4)]
    mmx = [ps1(f"p1_mmx{i}", [128, 512]) for i in range(2)]
    psT = [ps1(f"p1_psT{i}", [128, 128], BF16) for i in range(2)]

    def load_w(wdram):
        for dc in range(16):
            sch.dma("gpsimd", wsb[:, dc * 2048:(dc + 1) * 2048],
                    wdram[dc * 128:(dc + 1) * 128, :], inc="lw")
        return sch.cnt["lw"]

    if "k" in phases:
        load_w(wkT)
    sch.drain()
    sch.emit("consts")

    def ln_stats(pair, i, st):
        row = pair * 1024 + i * 128
        sl = i % 2
        xslot = xb[:, sl * 2048:(sl + 1) * 2048]
        dxs = f"dx{sl}"
        w = []
        if len(st["norm"]) >= 2:
            w.append(("dve", st["norm"][-2]))
        sch.dma("sync", xslot, x[row:row + 128, :], waits=w, inc=dxs)
        ldx = (dxs, sch.cnt[dxs])
        sch.add("scalar", lambda e, xs=xslot: e.activation(scr[:, :], xs, AF.Copy, accum_out=stat[:, 0:1]),
                waits=[ldx, ("dve", sch.cnt["dve"])], inc="act")
        sch.add("scalar", lambda e, xs=xslot: e.activation(scr[:, :], xs, AF.Square, accum_out=stat[:, 1:2]),
                waits=[], inc="act")
        sch.add("vector", lambda e: e.tensor_scalar(stat[:, 2:3], stat[:, 0:1], 1.0 / 2048.0, None, ALU.mult),
                waits=["act"], inc="dve")
        sch.add("vector", lambda e: e.tensor_tensor(stat[:, 3:4], stat[:, 2:3], stat[:, 2:3], ALU.mult),
                waits=["dve"], inc="dve")
        sch.add("vector", lambda e: e.tensor_scalar(stat[:, 4:5], stat[:, 1:2], 1.0 / 2048.0, None, ALU.mult),
                waits=["dve"], inc="dve")
        sch.add("vector", lambda e: e.tensor_tensor(stat[:, 5:6], stat[:, 4:5], stat[:, 3:4], ALU.subtract),
                waits=["dve"], inc="dve")
        sch.add("scalar", lambda e: e.activation(stat[:, 6:7], stat[:, 5:6], AF.Sqrt, bias=cst[:, 0:1]),
                waits=["dve"], inc="act")
        sch.add("vector", lambda e: e.reciprocal(stat[:, 7:8], stat[:, 6:7]),
                waits=["act"], inc="dve")
        sch.add("vector", lambda e: e.scalar_tensor_tensor(stat[:, 8:9], stat[:, 2:3], -1.0, stat[:, 7:8], ALU.mult, ALU.mult),
                waits=["dve"], inc="dve")
        w = ["dve"]
        if i in st["evacT"]:
            w.append(("pe", st["evacT"][i]))
        lnslot = lnbf[i]
        sch.add("vector", lambda e, xs=xslot, lnslot=lnslot: e.tensor_scalar(lnslot[:, :], xs, stat[:, 7:8], stat[:, 8:9], ALU.mult, ALU.add),
                waits=w, inc="dve")
        st["norm"].append(sch.cnt["dve"])
        st["lnr"][i] = sch.cnt["dve"]

    def ln_transpose(pair, i, lnT_free_pe, st):
        lb = lnT[pair % 2]
        lnslot = lnbf[i]
        lnT_free_st = sch.cnt["stln"]
        for dc in range(16):
            bk = dc % 2
            w = [("dve", st["lnr"][i])]
            if ("ev", bk) in st["evacT"]:
                w.append(st["evacT"][("ev", bk)])
            sch.add("tensor", lambda e, bk=bk, dc=dc, lnslot=lnslot: e.transpose(
                    psT[bk][:, :], lnslot[:, dc * 128:(dc + 1) * 128], ident_bf[:, :]),
                    waits=w, inc="pe")
            ew = [("pe", sch.cnt["pe"]), ("stln", lnT_free_st)]
            if lnT_free_pe > 0:
                ew.append(("pe", lnT_free_pe))
            sch.add("scalar", lambda e, bk=bk, dc=dc, i=i, lb=lb: e.activation(
                    lb[:, dc * 1024 + i * 128: dc * 1024 + i * 128 + 128], psT[bk][:, :], AF.Copy),
                    waits=ew, inc="act")
            st["evacT"][("ev", bk)] = ("act", sch.cnt["act"])
        st["evacT"][i] = sch.cnt["pe"]
        if i == 7:
            done = ("act", sch.cnt["act"])
            for half in range(2):
                qt = 2 * pair + half
                src = lb.ap().rearrange("p (c t) -> p c t", t=1024)[:, :, half * 512:(half + 1) * 512]
                dst = ln1T_d.ap()[qt].rearrange("(c p) t -> p c t", p=128)
                sch.dma("scalar", dst, src, waits=[done], inc="stln")
            st["done"][pair] = done

    def load_lnT(pair, free_pe):
        st0 = sch.cnt["stln"]
        lb = lnT[pair % 2]
        sps = f"sp{pair % 2}"
        for half in range(2):
            qt = 2 * pair + half
            dst = lb.ap().rearrange("p (c t) -> p c t", t=1024)[:, :, half * 512:(half + 1) * 512]
            src = ln1T_d.ap()[qt].rearrange("(c p) t -> p c t", p=128)
            w = [("pe", free_pe), ("stln", st0)] if free_pe > 0 else [("stln", st0)]
            sch.dma("sync", dst, src, waits=w, inc=sps)
        return (sps, sch.cnt[sps])

    def rope_evac(bank, h, pair, half, out_ap, stg_hist, rope_done, tabs):
        sl = h % 2
        sts = f"st{sl}"
        c0 = half * 512
        pev = sch.cnt["pe"]
        w0 = [("pe", pev), ("csld", tabs)]
        if stg_hist.get(sl, 0) > 0:
            w0.append((sts, stg_hist[sl]))
        sch.add("vector", lambda e: e.scalar_tensor_tensor(t1[:, :], bank[:, :], bcol[:, h:h + 1], csAp[:, c0:c0 + 512], ALU.add, ALU.mult), waits=w0, inc="dve")
        sch.add("vector", lambda e: e.scalar_tensor_tensor(t2[:, :], bank[:, :], bcol[:, h:h + 1], csBp[:, c0:c0 + 512], ALU.add, ALU.mult), inc="dve")
        rope_done[id(bank)] = sch.cnt["dve"]
        sch.add("vector", lambda e: e.tensor_copy(t3[0:64, :], t2[64:128, :]), waits=["dve"], inc="dve")
        sch.add("vector", lambda e: e.tensor_copy(t3[64:128, :], t2[0:64, :]), inc="dve")
        sch.add("vector", lambda e, sl=sl: e.tensor_tensor(stg[:, sl * 512:(sl + 1) * 512], t1[:, :], t3[:, :], ALU.add), waits=["dve"], inc="dve")
        sch.dma("scalar", out_ap, stg[:, sl * 512:(sl + 1) * 512], waits=["dve"], inc=sts)
        stg_hist[sl] = sch.cnt[sts]

    def proj_dmajor(out_dram, pairs, compute_ln):
        st = {"norm": [], "evacT": {}, "lnr": {}, "done": {}}
        stg_hist, rope_done = {}, {}
        lw = sch.cnt["lw"]
        mm_done = {}
        rope_last = {}
        if compute_ln:
            for i in range(8):
                ln_stats(pairs[0], i, st)
                ln_transpose(pairs[0], i, 0, st)
            for i in range(8):
                if len(pairs) > 1:
                    ln_stats(pairs[1], i, st)
        for pi, pair in enumerate(pairs):
            cw0 = [("dve", rope_last.get(pi - 1, 0))] if rope_last.get(pi - 1, 0) > 0 else []
            sch.dma("sync", csAp[:, :], csA_d[:, pair * 1024:(pair + 1) * 1024], waits=cw0, inc="csld")
            sch.dma("sync", csBp[:, :], csB_d[:, pair * 1024:(pair + 1) * 1024], inc="csld")
            tabs = sch.cnt["csld"]
            if compute_ln:
                lnT_ready = st["done"][pair]
            else:
                lnT_ready = load_lnT(pair, mm_done.get(pi - 2, 0))
            lb = lnT[pair % 2]
            weave = compute_ln and (pi + 1 < len(pairs))
            for h in range(16):
                banks = [mm4[(h % 2) * 2], mm4[(h % 2) * 2 + 1]]
                for dc in range(16):
                    for half in range(2):
                        bank = banks[half]
                        w = []
                        if dc == 0:
                            w = [("lw", lw), lnT_ready]
                            if rope_done.get(id(bank), 0) > 0:
                                w.append(("dve", rope_done[id(bank)]))
                        sch.add("tensor", lambda e, bank=bank, dc=dc, h=h, half=half, lb=lb:
                                e.matmul(bank[:, :], wsb[:, dc * 2048 + h * 128: dc * 2048 + h * 128 + 128],
                                         lb[:, dc * 1024 + half * 512: dc * 1024 + (half + 1) * 512],
                                         start=(dc == 0), stop=(dc == 15), skip_group_check=True),
                                waits=w, inc=("pe" if (dc == 15 and half == 1) else None))
                for half in range(2):
                    col0 = pair * 1024 + half * 512
                    rope_evac(banks[half], h, pair, half, out_dram.ap()[h][:, col0:col0 + 512], stg_hist, rope_done, tabs)
                if weave:
                    if h < 8:
                        ln_transpose(pairs[pi + 1], h, mm_done.get(pi - 1, 0), st)
                    elif pi + 2 < len(pairs):
                        ln_stats(pairs[pi + 2], h - 8, st)
            mm_done[pi] = sch.cnt["pe"]
            rope_last[pi] = sch.cnt["dve"]

    if "k" in phases:
        sch.begin()
        sch.dma("sync", bcol[:, :], bkr[:, :], inc="csld")
        proj_dmajor(k_d, list(range(8)), True)
        sch.drain()
        sch.emit("p1k")

    if "q" in phases:
        sch.begin()
        load_w(wqT)
        sch.dma("sync", bcol[:, :], bqr[:, :], inc="csld")
        proj_dmajor(q_d, [0, 1], False)
        sch.drain()
        sch.emit("p1q")

    if "v" in phases:
        sch.begin()
        lwv = load_w(wvT)
        st_hist = 0
        vmm_done = {}
        vbanks = [[mm4[0], mm4[1]], [mm4[2], mm4[3]], [mmx[0], mmx[1]]]
        vev = {}
        sthv = {}
        gi = 0
        for pair in range(8):
            lspill = load_lnT(pair, vmm_done.get(pair - 2, 0))
            lb = lnT[pair % 2]
            for ti in range(8):
                chunk = pair * 8 + ti
                for pp in range(2):
                    bp = vbanks[gi % 3]
                    w = [("lw", lwv), lspill]
                    if vev.get(gi % 3, 0) > 0:
                        w.append(("act", vev[gi % 3]))
                    for dc in range(16):
                        for dvh in range(2):
                            dv = pp * 2 + dvh
                            sch.add("tensor", lambda e, bk=bp[dvh], dc=dc, dv=dv, ti=ti, lb=lb:
                                    e.matmul(bk[:, :],
                                             lb[:, dc * 1024 + ti * 128: dc * 1024 + ti * 128 + 128],
                                             wsb[:, dc * 2048 + dv * 512: dc * 2048 + (dv + 1) * 512],
                                             start=(dc == 0), stop=(dc == 15), skip_group_check=True),
                                    waits=(w if (dc == 0 and dvh == 0) else []),
                                    inc=("pe" if (dc == 15 and dvh == 1) else None))
                    pev = sch.cnt["pe"]
                    vst = f"st{2 + gi % 2}"
                    for dvh in range(2):
                        dv = pp * 2 + dvh
                        w2 = [("pe", pev)]
                        if sthv.get(gi % 2, 0) > 0:
                            w2.append((vst, sthv[gi % 2]))
                        sch.add("scalar", lambda e, dv=dv, bk=bp[dvh]: e.activation(
                                lnbf[dv][:, 0:512], bk[:, :], AF.Copy),
                                waits=(w2 if dvh == 0 else []), inc="act")
                    vev[gi % 3] = sch.cnt["act"]
                    evdone = sch.cnt["act"]
                    for dvh in range(2):
                        dv = pp * 2 + dvh
                        for qhh in range(4):
                            hh = dv * 4 + qhh
                            sch.dma("scalar", v_d.ap()[hh][:, chunk, :],
                                    lnbf[dv][:, qhh * 128:(qhh + 1) * 128],
                                    waits=[("act", evdone)], inc=vst)
                    sthv[gi % 2] = sch.cnt[vst]
                    gi += 1
            vmm_done[pair] = sch.cnt["pe"]
        sch.drain()
        sch.emit("p1v")

    p1s.close()

    catt = contextlib.ExitStack()
    C_all = catt.enter_context(nc.sbuf_tensor("C_all", [128, H * ROWS], BF16))
    if "att" in phases:
        with contextlib.ExitStack() as p2s:
            sb2 = lambda n, shp, dt: p2s.enter_context(nc.sbuf_tensor(n, shp, dt))
            ps2 = lambda n, shp, dt=F32: p2s.enter_context(nc.psum_tensor(n, shp, dt))
            kh = [sb2(f"p2_k{i}", [128, S], BF16) for i in range(2)]
            vh = [sb2(f"p2_v{i}", [128, S], BF16) for i in range(2)]
            qh = [sb2(f"p2_q{i}", [128, ROWS], BF16) for i in range(2)]
            pt_b = [sb2(f"p2_pt{i}", [128, 512], BF16) for i in range(3)]
            mk_sb = sb2("p2_masks", [128, 4 * 512], F32)
            sbsc = sb2("p2_sbsc", [128, NREST], F32)
            sbbi = sb2("p2_sbbi", [128, NREST], F32)
            rsum = sb2("p2_rsum", [1, 512], F32)
            rb = sb2("p2_rb", [128, 512], F32)
            ct = sb2("p2_ct", [128, 512], F32)
            bvc = sb2("p2_bvc", [128, 16], F32)
            psS = [ps2(f"p2_psS{i}", [128, 512]) for i in range(3)]
            psC = [ps2(f"p2_psC{i}", [128, 512]) for i in range(2)]
            psR = [ps2(f"p2_psR{i}", [128, 512]) for i in range(2)]
            psBC = ps2("p2_psBC", [128, 512])

            sch.begin()
            sch.dma("sync", mk_sb[:, :], masks_d[:, :], inc="dmc")
            sch.dma("sync", bvc[:, :], bvr_d[:, :], inc="dmc")
            sch.dma("sync", sbsc[:, :], sb_sc_d[:, :], inc="dmc")
            sch.dma("sync", sbbi[:, :], sb_bi_d[:, :], inc="dmc")
            tabs = sch.cnt["dmc"]
            kvq_done = {}
            cprev = {}
            ld = {}
            pend_epi = [None]

            def load_head(h):
                i = h % 2
                hs = f"hd{i}"
                w = []
                if kvq_done.get(h - 2, 0) > 0:
                    w = [("pe", kvq_done[h - 2])]
                sch.dma("sync", kh[i][:, :], k_d.ap()[h], waits=w, inc=hs)
                sch.dma("sync", vh[i][:, :], v_d.ap()[h].rearrange("p c d -> p (c d)"), inc=hs)
                sch.dma("sync", qh[i][:, :], q_d.ap()[h], inc=hs)
                return (hs, sch.cnt[hs])

            ld[0] = load_head(0)
            ld[1] = load_head(1)
            for h in range(16):
                i = h % 2
                for s in range(4):
                    g = (h * 4 + s) % 2
                    bC = psC[g]
                    bR = psR[g]
                    units = slot_units(s)
                    nu = len(units)
                    exp_done = {}
                    sv = {}

                    def emit_S(u):
                        kind, kt, aux = units[u]
                        bS = psS[u % 3]
                        if u < 3:
                            w = [ld[h], ("act", sch.cnt["act"])]
                        else:
                            w = [("act", exp_done[u - 3])]
                        sch.add("tensor", lambda e, bS=bS, kt=kt, i=i, s=s:
                                e.matmul(bS[:, :], kh[i][:, kt * 128:(kt + 1) * 128],
                                         qh[i][:, s * 512:(s + 1) * 512], start=True, stop=True,
                                         skip_group_check=True),
                                waits=w, inc="pe")
                        sv[u] = sch.cnt["pe"]

                    for uu in range(min(3, nu)):
                        emit_S(uu)
                    if pend_epi[0] is not None:
                        pend_epi[0]()
                        pend_epi[0] = None
                    for u, (kind, kt, aux) in enumerate(units):
                        bS = psS[u % 3]
                        if kind == "diag":
                            sch.add("vector", lambda e, bS=bS, aux=aux: e.tensor_tensor(
                                    bS[:, :], bS[:, :], mk_sb[:, aux * 512:(aux + 1) * 512], ALU.add),
                                    waits=[("pe", sv[u]), ("dmc", tabs)], inc="dve")
                            mw = [("dve", sch.cnt["dve"])]
                        else:
                            mw = [("pe", sv[u])]
                        ptb = pt_b[u % 3]
                        if kind == "rest":
                            sch.add("scalar", lambda e, bS=bS, ptb=ptb, aux=aux:
                                    e.activation(ptb[:, :], bS[:, :], AF.Exp,
                                                 bias=sbbi[:, aux:aux + 1], scale=sbsc[:, aux:aux + 1]),
                                    waits=mw + [("dmc", tabs)], inc="act")
                        else:
                            sch.add("scalar", lambda e, bS=bS, ptb=ptb:
                                    e.activation(ptb[:, :], bS[:, :], AF.Exp, bias=cst[:, 1:2], scale=cst[:, 2:3]),
                                    waits=mw, inc="act")
                        exp_done[u] = sch.cnt["act"]
                        wpv = [("act", exp_done[u])]
                        if u == 0 and cprev.get(g, 0) > 0:
                            wpv.append(("dve", cprev[g]))
                        sch.add("tensor", lambda e, ptb=ptb, kt=kt, i=i, bC=bC, first=(u == 0), last=(u == nu - 1):
                                e.matmul(bC[:, :], vh[i][:, kt * 128:(kt + 1) * 128], ptb[:, :],
                                         start=first, stop=last, skip_group_check=True),
                                waits=wpv, inc=None)
                        sch.add("tensor", lambda e, ptb=ptb, bR=bR, first=(u == 0), last=(u == nu - 1):
                                e.matmul(bR[0:1, :], ones_bf[:, :], ptb[:, :],
                                         start=first, stop=last, skip_group_check=True),
                                inc="pe")
                        if u + 3 < nu:
                            emit_S(u + 3)
                    pv_done = sch.cnt["pe"]

                    def epi(bC=bC, bR=bR, h=h, s=s, g=g, pv_done=pv_done):
                        sch.add("vector", lambda e: e.reciprocal(rsum[0:1, :], bR[0:1, :]),
                                waits=[("pe", pv_done)], inc="dve")
                        sch.add("tensor", lambda e: e.matmul(psBC[:, :], ones1f[0:1, :], rsum[0:1, :],
                                                             start=True, stop=True, skip_group_check=True),
                                waits=["dve"], inc="pe")
                        sch.add("vector", lambda e: e.tensor_copy(rb[:, :], psBC[:, :]),
                                waits=["pe"], inc="dve")
                        sch.add("vector", lambda e: e.tensor_tensor(
                                ct[:, :], bC[:, :], rb[:, :], ALU.mult),
                                waits=["dve"], inc="dve")
                        sch.add("vector", lambda e: e.tensor_scalar(
                                C_all[:, h * ROWS + s * 512: h * ROWS + (s + 1) * 512], ct[:, :], bvc[:, h:h + 1], None, ALU.add),
                                waits=["dve"], inc="dve")
                        cprev[g] = sch.cnt["dve"]
                    pend_epi[0] = epi
                kvq_done[h] = sch.cnt["pe"]
                if h + 2 < 16:
                    ld[h + 2] = load_head(h + 2)
            if pend_epi[0] is not None:
                pend_epi[0]()
                pend_epi[0] = None
            if dbg is not None and dbg[0] == "C_all":
                sch.dma("gpsimd", c_d.ap(), C_all.ap(), waits=[("dve", sch.cnt["dve"])], inc="dmg")
            sch.drain()
            sch.emit("p2")

    if "oproj" in phases:
        with contextlib.ExitStack() as p3s:
            sb3 = lambda n, shp, dt: p3s.enter_context(nc.sbuf_tensor(n, shp, dt))
            ps3 = lambda n, shp, dt=F32: p3s.enter_context(nc.psum_tensor(n, shp, dt))
            wosb = sb3("p3_wo", [128, 16 * 2048], BF16)
            ident_f = sb3("p3_identf", [128, 128], F32)
            xb3 = sb3("p3_x", [128, 2 * 2048], F32)
            xp3 = sb3("p3_xp", [128, 2048], F32)
            st3 = sb3("p3_stat", [128, 16], F32)
            ln3f_a = sb3("p3_lnfa", [128, 2048], F32)
            ln3f_b = sb3("p3_lnfb", [128, 2048], F32)
            tp3 = sb3("p3_tp", [128, 32], F32)
            lnTs = sb3("p3_lnTs", [128, 16 * 128], F32)
            lnTq = sb3("p3_lnTq", [128, 16 * 512], BF16)
            bob = sb3("p3_bob", [128, D], F32)
            gsb = sb3("p3_g", [128, 128], F32)
            psO = [ps3(f"p3_psO{i}", [128, 512]) for i in range(4)]
            psT3 = [ps3(f"p3_psT{i}", [128, 128]) for i in range(2)]
            psG = ps3("p3_psG", [128, 8])

            sch.begin()
            for dc in range(16):
                sch.dma("gpsimd", wosb[:, dc * 2048:(dc + 1) * 2048],
                        woT[dc * 128:(dc + 1) * 128, :], inc="lw")
            lwo = sch.cnt["lw"]
            sch.dma("sync", bob[:, :], bo_bc_d[:, :], inc="dmc")
            sch.dma("sync", gsb[:, :], gate_r[:, :], inc="dmc")
            sch.dma("sync", ident_f[:, :], ident_d[:, :], inc="dmc")
            tb3 = sch.cnt["dmc"]
            xread, xpst, lnq_st, tev = {}, {}, 0, {}
            ln3f = [ln3f_a, ln3f_b]
            lnfd, gstate = {}, {}
            for tt in range(17):
                if tt < 16:
                    sl = tt % 2
                    xslot = xb3[:, sl * 2048:(sl + 1) * 2048]
                    dxs = f"dx{sl}"
                    w = []
                    if xread.get(tt - 2, 0) > 0:
                        w = [("dve", xread[tt - 2])]
                    sch.dma("sync", xslot, x[tt * 128:(tt + 1) * 128, :], waits=w, inc=dxs)
                    ldx = (dxs, sch.cnt[dxs])
                    evd = sch.cnt["dve"]
                    for ic in range(16):
                        for oc in range(4):
                            w = []
                            if ic == 0 and oc == 0:
                                w = [("lw", lwo), ("dve", evd)]
                            sch.add("tensor", lambda e, ic=ic, oc=oc, tt=tt:
                                    e.matmul(psO[oc][:, :],
                                             C_all[:, ic * ROWS + tt * 128: ic * ROWS + (tt + 1) * 128],
                                             wosb[:, ic * 2048 + oc * 512: ic * 2048 + (oc + 1) * 512],
                                             start=(ic == 0), stop=(ic == 15), skip_group_check=True),
                                    waits=w, inc=("pe" if (ic == 15 and oc == 3) else None))
                    odone = sch.cnt["pe"]
                    for oc in range(4):
                        w = [("pe", odone), ldx, ("dmc", tb3)]
                        if xpst.get(tt - 1, 0) > 0:
                            w.append(("stx", xpst[tt - 1]))
                        sch.add("vector", lambda e, oc=oc, xslot=xslot: e.tensor_tensor(
                                xp3[:, oc * 512:(oc + 1) * 512], psO[oc][:, :], xslot[:, oc * 512:(oc + 1) * 512], ALU.add),
                                waits=(w if oc == 0 else []), inc="dve")
                        sch.add("vector", lambda e, oc=oc: e.tensor_tensor(
                                xp3[:, oc * 512:(oc + 1) * 512], xp3[:, oc * 512:(oc + 1) * 512], bob[:, oc * 512:(oc + 1) * 512], ALU.add),
                                waits=["dve"], inc="dve")
                    xread[tt] = sch.cnt["dve"]
                    xr = sch.cnt["dve"]
                    sch.dma("scalar", xp_d[tt * 128:(tt + 1) * 128, :], xp3[:, :], waits=[("dve", xr)], inc="stx")
                    xpst[tt] = sch.cnt["stx"]
                    lf = ln3f[tt % 2]
                    sch.add("scalar", lambda e, lf=lf: e.activation(lf[:, :], xp3[:, :], AF.Copy, accum_out=st3[:, 0:1]),
                            waits=[("dve", xr), ("dve", tev.get(tt - 1, 0)), ("pe", sch.cnt["pe"])], inc="act")
                    sch.add("scalar", lambda e, lf=lf: e.activation(lf[:, :], xp3[:, :], AF.Square, accum_out=st3[:, 1:2]),
                            waits=[], inc="act")
                    sch.add("vector", lambda e: e.tensor_scalar(st3[:, 2:3], st3[:, 0:1], 1.0 / 2048.0, None, ALU.mult), waits=["act"], inc="dve")
                    sch.add("vector", lambda e: e.tensor_tensor(st3[:, 3:4], st3[:, 2:3], st3[:, 2:3], ALU.mult), waits=["dve"], inc="dve")
                    sch.add("vector", lambda e: e.tensor_scalar(st3[:, 4:5], st3[:, 1:2], 1.0 / 2048.0, None, ALU.mult), waits=["dve"], inc="dve")
                    sch.add("vector", lambda e: e.tensor_tensor(st3[:, 5:6], st3[:, 4:5], st3[:, 3:4], ALU.subtract), waits=["dve"], inc="dve")
                    sch.add("scalar", lambda e: e.activation(st3[:, 6:7], st3[:, 5:6], AF.Sqrt, bias=cst[:, 0:1]), waits=["dve"], inc="act")
                    sch.add("vector", lambda e: e.reciprocal(st3[:, 7:8], st3[:, 6:7]), waits=["act"], inc="dve")
                    sch.add("vector", lambda e: e.scalar_tensor_tensor(st3[:, 8:9], st3[:, 2:3], -1.0, st3[:, 7:8], ALU.mult, ALU.mult), waits=["dve"], inc="dve")
                    sch.add("vector", lambda e, lf=lf: e.tensor_scalar(lf[:, :], xp3[:, :], st3[:, 7:8], st3[:, 8:9], ALU.mult, ALU.add), waits=["dve"], inc="dve")
                    lnfd[tt] = ("dve", sch.cnt["dve"])
                pt_ = tt - 1
                if pt_ < 0:
                    continue
                lf = ln3f[pt_ % 2]
                for dc in range(16):
                    bk = dc % 2
                    w = [lnfd[pt_]]
                    if tev.get(("b", bk), 0) > 0:
                        w.append(("dve", tev[("b", bk)]))
                    if tev.get(("a", bk), 0) > 0:
                        w.append(("act", tev[("a", bk)]))
                    sch.add("tensor", lambda e, bk=bk, dc=dc, lf=lf: e.transpose(psT3[bk][:, :], lf[:, dc * 128:(dc + 1) * 128], ident_f[:, :]),
                            waits=w, inc="pe")
                    pev = sch.cnt["pe"]
                    sch.add("scalar", lambda e, bk=bk, dc=dc: e.activation(lnTs[:, dc * 128:(dc + 1) * 128], psT3[bk][:, :], AF.Copy),
                            waits=[("pe", pev)], inc="act")
                    tev[("a", bk)] = sch.cnt["act"]
                    w = [("act", sch.cnt["act"])]
                    if lnq_st > 0:
                        w.append(("stln", lnq_st))
                    sch.add("vector", lambda e, dc=dc, pt_=pt_: e.tensor_copy(
                            lnTq[:, dc * 512 + (pt_ % 4) * 128: dc * 512 + (pt_ % 4) * 128 + 128],
                            lnTs[:, dc * 128:(dc + 1) * 128]),
                            waits=w, inc="dve")
                    tev[("b", bk)] = sch.cnt["dve"]
                tev[pt_] = sch.cnt["dve"]
                tdone_act = sch.cnt["act"]
                for dc in range(16):
                    sch.add("tensor", lambda e, dc=dc: e.matmul(psG[:, :], lnTs[:, dc * 128:(dc + 1) * 128],
                                                                gsb[:, dc * 8:(dc + 1) * 8],
                                                                start=(dc == 0), stop=(dc == 15), skip_group_check=True),
                            waits=([("act", tdone_act), ("dve", sch.cnt["dve"])] if dc == 0 else []),
                            inc=("pe" if dc == 15 else None))
                gd = sch.cnt["pe"]
                sch.add("vector", lambda e: e.reduce_max(st3[:, 9:10], psG[:, :], axis=AX.X), waits=[("pe", gd)], inc="dve")
                sch.add("vector", lambda e: e.tensor_scalar(tp3[:, 0:8], psG[:, :], st3[:, 9:10], None, ALU.is_ge), waits=["dve"], inc="dve")
                sch.add("vector", lambda e: e.scalar_tensor_tensor(tp3[:, 8:16], tp3[:, 0:8], cst[:, 3:4], psG[:, :], ALU.mult, ALU.add), waits=["dve"], inc="dve")
                sch.add("vector", lambda e: e.reduce_max(st3[:, 10:11], tp3[:, 8:16], axis=AX.X), waits=["dve"], inc="dve")
                sch.add("vector", lambda e: e.tensor_scalar(tp3[:, 16:24], tp3[:, 8:16], st3[:, 10:11], None, ALU.is_ge), waits=["dve"], inc="dve")
                sch.add("vector", lambda e: e.tensor_tensor(st3[:, 11:12], st3[:, 10:11], st3[:, 9:10], ALU.subtract), waits=["dve"], inc="dve")
                sch.add("scalar", lambda e: e.activation(st3[:, 12:13], st3[:, 11:12], AF.Sigmoid), waits=["dve"], inc="act")
                sch.add("vector", lambda e: e.tensor_scalar(st3[:, 13:14], st3[:, 12:13], -1.0, 1.0, ALU.mult, ALU.add), waits=["act"], inc="dve")
                sch.add("vector", lambda e: e.tensor_scalar(tp3[:, 24:32], tp3[:, 0:8], st3[:, 13:14], None, ALU.mult), waits=["dve"], inc="dve")
                sch.add("vector", lambda e, pt_=pt_: e.scalar_tensor_tensor(
                        cw_sb[:, pt_ * 8:(pt_ + 1) * 8], tp3[:, 16:24], st3[:, 12:13], tp3[:, 24:32], ALU.mult, ALU.add),
                        waits=["dve"], inc="dve")
                if pt_ % 4 == 3:
                    qt = pt_ // 4
                    src = lnTq.ap().rearrange("p (c t) -> p c t", t=512)
                    dst = ln2T_d.ap()[qt].rearrange("(c p) t -> p c t", p=128)
                    sch.dma("scalar", dst, src, waits=[("dve", sch.cnt["dve"])], inc="stln")
                    lnq_st = sch.cnt["stln"]
            sch.drain()
            sch.emit("p3")
    catt.close()

    if "ffn" in phases:
        with contextlib.ExitStack() as p4s:
            sb4 = lambda n, shp, dt: p4s.enter_context(nc.sbuf_tensor(n, shp, dt))
            ps4 = lambda n, shp, dt=F32: p4s.enter_context(nc.psum_tensor(n, shp, dt))
            ln2h = sb4("p4_ln2", [128, 16 * 1024], BF16)
            wbuf = sb4("p4_w", [128, 32 * 1024], BF16)
            hbuf = sb4("p4_h", [128, 16 * 1024], BF16)
            acc = sb4("p4_acc", [128, 8 * 2048], F32)
            silu_t = sb4("p4_silu", [128, 512], BF16)
            psF = [ps4(f"p4_psF{i}", [128, 512]) for i in range(8)]

            sch.begin()
            for half in range(2):
                for qi in range(2):
                    qt = half * 2 + qi
                    dst = ln2h.ap().rearrange("p (c t) -> p c t", t=1024)[:, :, qi * 512:(qi + 1) * 512]
                    src = ln2T_d.ap()[qt].rearrange("(c p) t -> p c t", p=128)
                    sch.dma("sync", dst, src, waits=[("pe", sch.cnt["pe"])], inc="spill")
                for t8 in range(8):
                    row = half * 1024 + t8 * 128
                    sch.dma("sync", acc[:, t8 * 2048:(t8 + 1) * 2048], xp_d[row:row + 128, :],
                            waits=[("sto", sch.cnt["sto"]), ("dve", sch.cnt["dve"])], inc="spill")
                base = sch.cnt["spill"]

                ring_free = {-2: sch.cnt["pe"], -1: sch.cnt["pe"]}

                def load_wi(it, wdram, e_, mh):
                    slot = it % 2
                    wrs = f"wr{slot}"
                    w = [("pe", ring_free[it - 2])] if ring_free.get(it - 2, 0) > 0 else []
                    for dc in range(16):
                        sch.dma("gpsimd", wbuf[:, slot * 16384 + dc * 1024: slot * 16384 + (dc + 1) * 1024],
                                wdram.ap()[e_][dc * 128:(dc + 1) * 128, mh * 1024:(mh + 1) * 1024],
                                waits=(w if dc == 0 else []), inc=wrs)
                    return (wrs, sch.cnt[wrs])

                def load_wo(it, e_, mh):
                    slot = it % 2
                    wrs = f"wr{slot}"
                    w = [("pe", ring_free[it - 2])] if ring_free.get(it - 2, 0) > 0 else []
                    for mD in range(8):
                        m0 = mh * 8 + mD
                        sch.dma("gpsimd", wbuf[:, slot * 16384 + mD * 2048: slot * 16384 + (mD + 1) * 2048],
                                wof.ap()[e_][m0 * 128:(m0 + 1) * 128, :],
                                waits=(w if mD == 0 else []), inc=wrs)
                    return (wrs, sch.cnt[wrs])

                plan = []
                for e_ in range(E):
                    plan += [("a0", e_, 0), ("a0", e_, 1), ("a1", e_, 0), ("a1", e_, 1),
                             ("b", e_, 0), ("b", e_, 1)]
                for it, (kindp, e_, mh) in enumerate(plan):
                    if kindp == "a0":
                        lw_t = load_wi(it, wi0, e_, mh)
                    elif kindp == "a1":
                        lw_t = load_wi(it, wi1, e_, mh)
                    else:
                        lw_t = load_wo(it, e_, mh)
                    slot = it % 2
                    if kindp in ("a0", "a1"):
                        hev = {}
                        for mc in range(8):
                            m_abs = mh * 8 + mc
                            bks = [psF[(mc % 4) * 2], psF[(mc % 4) * 2 + 1]]
                            w = [lw_t, ("spill", base)]
                            if mc == 0:
                                w += [("act", sch.cnt["act"]), ("dve", sch.cnt["dve"])]
                            if mc >= 4:
                                w.append(("act" if kindp == "a0" else "dve",
                                          hev.get((mc % 4), 0)))
                            for dc in range(16):
                                for tc in range(2):
                                    sch.add("tensor", lambda e, bank=bks[tc], dc=dc, mc=mc, tc=tc, slot=slot:
                                            e.matmul(bank[:, :],
                                                     wbuf[:, slot * 16384 + dc * 1024 + mc * 128: slot * 16384 + dc * 1024 + mc * 128 + 128],
                                                     ln2h[:, dc * 1024 + tc * 512: dc * 1024 + (tc + 1) * 512],
                                                     start=(dc == 0), stop=(dc == 15), skip_group_check=True),
                                            waits=(w if (dc == 0 and tc == 0) else []),
                                            inc=("pe" if (dc == 15 and tc == 1) else None))
                            hd = sch.cnt["pe"]
                            for tc in range(2):
                                col = m_abs * 1024 + tc * 512
                                if kindp == "a0":
                                    sch.add("scalar", lambda e, bank=bks[tc], col=col: e.activation(
                                            hbuf[:, col:col + 512], bank[:, :], AF.Copy),
                                            waits=[("pe", hd)], inc="act")
                                    hev[mc % 4] = sch.cnt["act"]
                                else:
                                    sch.add("scalar", lambda e, col=col: e.activation(
                                            silu_t[:, :], hbuf[:, col:col + 512], AF.Silu),
                                            waits=[("pe", hd), ("dve", sch.cnt["dve"])], inc="act")
                                    sch.add("vector", lambda e, bank=bks[tc], col=col: e.tensor_tensor(
                                            hbuf[:, col:col + 512], silu_t[:, :], bank[:, :], ALU.mult),
                                            waits=["act"], inc="dve")
                                    hev[mc % 4] = sch.cnt["dve"]
                        ring_free[it] = sch.cnt["pe"]
                    else:
                        bev = {}
                        for t8 in range(8):
                            w0 = [lw_t, ("dve", bev.get(t8 % 2, sch.cnt["dve"]))]
                            if t8 == 0:
                                w0.append(("dve", sch.cnt["dve"]))
                            bset = [psF[(t8 % 2) * 4 + k] for k in range(4)]
                            for mD in range(8):
                                for oc in range(4):
                                    sch.add("tensor", lambda e, t8=t8, mD=mD, oc=oc, slot=slot, mh=mh, bset=bset:
                                            e.matmul(bset[oc][:, :],
                                                     hbuf[:, (mh * 8 + mD) * 1024 + t8 * 128: (mh * 8 + mD) * 1024 + t8 * 128 + 128],
                                                     wbuf[:, slot * 16384 + mD * 2048 + oc * 512: slot * 16384 + mD * 2048 + (oc + 1) * 512],
                                                     start=(mD == 0), stop=(mD == 7), skip_group_check=True),
                                            waits=(w0 if (mD == 0 and oc == 0) else []),
                                            inc=("pe" if (mD == 7 and oc == 3) else None))
                            bd = sch.cnt["pe"]
                            for oc in range(4):
                                sch.add("vector", lambda e, t8=t8, oc=oc, e_=e_, half=half, bset=bset: e.scalar_tensor_tensor(
                                        acc[:, t8 * 2048 + oc * 512: t8 * 2048 + (oc + 1) * 512],
                                        bset[oc][:, :],
                                        cw_sb[:, (half * 8 + t8) * 8 + e_:(half * 8 + t8) * 8 + e_ + 1],
                                        acc[:, t8 * 2048 + oc * 512: t8 * 2048 + (oc + 1) * 512],
                                        ALU.mult, ALU.add),
                                        waits=([("pe", bd)] if oc == 0 else []), inc="dve")
                            bev[t8 % 2] = sch.cnt["dve"]
                        ring_free[it] = sch.cnt["pe"]
                accd = sch.cnt["dve"]
                for t8 in range(8):
                    row = half * 1024 + t8 * 128
                    sch.dma("scalar", out[row:row + 128, :], acc[:, t8 * 2048:(t8 + 1) * 2048],
                            waits=[("dve", accd)], inc="sto")
            sch.drain()
            sch.emit("p4")

    if dbg is not None and dbg[0] == "cw":
        sch.begin()
        sch.dma("gpsimd", cw_d.ap(), cw_sb.ap(), inc="dmg")
        sch.dma("gpsimd", dbg_out.ap(), cw_d.ap(), waits=[("dmg", sch.cnt["dmg"])], inc="dmg")
        sch.wait_only("gpsimd", [("dmg", sch.cnt["dmg"])])
        sch.emit("dbgblk")
        dbg = None
    if dbg is not None:
        name = dbg[0]
        srcs = {"ln1T_d": ln1T_d, "q_d": q_d, "k_d": k_d, "v_d": v_d,
                "ln2T_d": ln2T_d, "xp_d": xp_d, "C_all": c_d}
        src = srcs[name]
        sch.begin()
        sch.dma("gpsimd", dbg_out.ap(), src.ap(), inc="dmg")
        sch.wait_only("gpsimd", [("dmg", sch.cnt["dmg"])])
        sch.emit("dbgblk")

    return nc



def _rope_tables():
    inv = (1.0 / (10000.0 ** (np.arange(0, DH, 2, dtype=np.float32) / np.float32(DH)))).astype(np.float32)
    t = np.arange(S, dtype=np.float32)
    fr = t[None, :] * inv[:, None]
    return np.cos(fr).astype(np.float32), np.sin(fr).astype(np.float32)


def core_perm(core):
    j = core % 4
    chunks = own_chunks(j)
    own = np.concatenate([np.arange(c * 512, (c + 1) * 512) for c in chunks])
    rest = np.setdiff1d(np.arange(S), own)
    return np.concatenate([own, rest]), chunks


def make_inputs(core, full):
    b = core // 4
    perm, chunks = core_perm(core)

    cos, sin = _rope_tables()
    csA = np.concatenate([cos[:, perm], cos[:, perm]], axis=0)
    csB = np.concatenate([sin[:, perm], -sin[:, perm]], axis=0)

    n_s = [4 * (chunks[s] - s) for s in range(4)]
    sc = np.zeros(NREST, np.float32)
    bi = np.zeros(NREST, np.float32)
    for s in range(4):
        for r in range(NMAX[s]):
            u = CUM[s] + r
            if r < n_s[s]:
                sc[u] = SCALE
                bi[u] = -CEXP
            else:
                sc[u] = 0.0
                bi[u] = -100.0
    sb_scale = np.ascontiguousarray(np.broadcast_to(sc, (128, NREST)))
    sb_bias = np.ascontiguousarray(np.broadcast_to(bi, (128, NREST)))

    mk = np.zeros((128, 4, 512), np.float32)
    kk = np.arange(128)[:, None]
    qq = np.arange(512)[None, :]
    for dk in range(4):
        mk[:, dk, :] = np.where(qq >= dk * 128 + kk, 0.0, -1e9)
    masks = np.ascontiguousarray(mk.reshape(128, 4 * 512))

    d = {
        "x": np.ascontiguousarray(full["L0_attn_hidden_states"][b][perm]),
        "csA": np.ascontiguousarray(csA),
        "csB": np.ascontiguousarray(csB),
        "wqT": np.ascontiguousarray(full["L0_attn_Wq"].T),
        "wkT": np.ascontiguousarray(full["L0_attn_Wk"].T),
        "wvT": np.ascontiguousarray(full["L0_attn_Wv"].T),
        "woT": np.ascontiguousarray(full["L0_attn_Wo"].T),
        "bqr": np.ascontiguousarray(full["L0_attn_bq"].reshape(16, 128).T),
        "bkr": np.ascontiguousarray(full["L0_attn_bk"].reshape(16, 128).T),
        "bvr": np.ascontiguousarray(full["L0_attn_bv"].reshape(16, 128).T),
        "bo_bc": np.ascontiguousarray(np.broadcast_to(full["L0_attn_bo"], (128, D))),
        "gate_r": np.ascontiguousarray(
            full["L0_ffn_gate_kernel"].reshape(16, 128, 8).transpose(1, 0, 2).reshape(128, 128)),
        "wi0": np.ascontiguousarray(full["L0_ffn_wi_0"]),
        "wi1": np.ascontiguousarray(full["L0_ffn_wi_1"]),
        "wof": np.ascontiguousarray(full["L0_ffn_wo"]),
        "sb_scale": sb_scale,
        "sb_bias": sb_bias,
        "masks": masks,
        "ident": np.eye(128, dtype=np.float32),
        "ones_in": np.ones((2, 128), np.float32),
        "constv": np.ascontiguousarray(np.broadcast_to(
            np.array([EPS, -CEXP, SCALE, -1e9, 1.0 / 2048.0, 0, 0, 0], np.float32), (128, 8))),
    }
    return d


def kernel(**inputs):
    full = {k: np.asarray(v) for k, v in inputs.items()}
    nc = build_program()
    in_maps = [make_inputs(core, full) for core in range(8)]
    res = run_bass_kernel_spmd(nc, in_maps, core_ids=list(range(8)))
    outp = np.zeros((B, S, D), np.float32)
    for core in range(8):
        b = core // 4
        perm, _ = core_perm(core)
        outp[b][perm[:ROWS]] = res.results[core]["out"]
    return outp


# version 11
# speedup vs baseline: 1.0519x; 1.0264x over previous
import contextlib
import numpy as np
import concourse.bass as bass
import concourse.mybir as mybir
from concourse.bass_utils import run_bass_kernel_spmd

F32 = mybir.dt.float32
BF16 = mybir.dt.bfloat16
AF = mybir.ActivationFunctionType
ALU = mybir.AluOpType
AX = mybir.AxisListType

B, S, D, H, DH, E, M = 2, 8192, 2048, 16, 128, 8, 2048
ROWS = 2048
NQT = S // 512
SCALE = float(np.float32(1.0) / np.float32(np.sqrt(np.float32(128.0))))
CEXP = 12.0
NMAX = [12, 24, 36, 48]
CUM = [0, 12, 36, 72]
NREST = 120
EPS = 1e-5
ENGS = ("sync", "gpsimd", "scalar", "vector", "tensor")
P3_LEVEL = 4

DMA_SEMS = ("dmc", "dmg", "dx0", "dx1", "lw", "wr0", "wr1", "spill", "sp0", "sp1", "csld",
            "hd0", "hd1", "st0", "st1", "st2", "st3", "stv", "stln", "stx", "sto",
            "xb0", "xb1")


def own_chunks(j):
    return sorted({j, 7 - j, 8 + j, 15 - j})


def slot_units(s):
    units = []
    for t in range(4 * s):
        units.append(("own", t, None))
    for dk in range(4):
        units.append(("diag", 4 * s + dk, dk))
    for r in range(NMAX[s]):
        units.append(("rest", 16 + r, CUM[s] + r))
    return units


class Sch:

    def __init__(self, nc):
        self.nc = nc
        self.stack = contextlib.ExitStack()
        self.sem = {}
        self.cnt = {}
        self.q = None

    def sems(self, *names):
        for n in names:
            self.sem[n] = self.stack.enter_context(self.nc.semaphore(n))
            self.cnt[n] = 0

    def begin(self):
        assert self.q is None
        self.q = {e: [] for e in ENGS}

    def add(self, eng, fn, waits=(), inc=None, by=1, auto=True):
        wd = {}
        for it in waits:
            if isinstance(it, tuple):
                nm, val = it
            else:
                nm, val = it, self.cnt[it]
            wd[nm] = max(wd.get(nm, 0), val)
        if auto and eng == "vector":
            wd["dve"] = max(wd.get("dve", 0), self.cnt["dve"])
        if auto and eng == "scalar":
            wd["act"] = max(wd.get("act", 0), self.cnt["act"])
        w = [(nm, val) for nm, val in wd.items() if val > 0]
        if inc is not None:
            self.cnt[inc] += by
        self.q[eng].append((fn, w, inc, by))

    def dma(self, eng, out, in_, waits=(), inc=None, by=16, **kw):
        assert inc is not None
        self.add(eng, lambda e: e.dma_start(out, in_, **kw), waits, inc, by, auto=False)

    def wait_only(self, eng, waits):
        self.add(eng, lambda e: None, waits, None)

    def drain(self):
        self.wait_only("gpsimd", [(n, self.cnt[n]) for n in DMA_SEMS])

    def emit(self, name):
        q, self.q = self.q, None
        sem = self.sem

        def mk(eng_name):
            ops = q[eng_name]

            def body(e):
                for fn, w, inc, by in ops:
                    for sname, val in w:
                        e.wait_ge(sem[sname], val)
                    ins = fn(e)
                    if inc is not None:
                        assert ins is not None
                        ins.then_inc(sem[inc], by)
            return body
        with self.nc.Block(name) as blk:
            if q["gpsimd"]:
                blk.gpsimd(mk("gpsimd"))
            if q["scalar"]:
                blk.scalar(mk("scalar"))
            if q["tensor"]:
                blk.tensor(mk("tensor"))
            if q["vector"]:
                blk.vector(mk("vector"))
            if q["sync"]:
                blk.sync(mk("sync"))


def build_program(phases=("k", "q", "v", "att", "oproj", "ffn"), dbg=None):
    nc = bass.Bass("TRN2", target_bir_lowering=False)
    sch = Sch(nc)
    sb = lambda name, shape, dt: sch.stack.enter_context(nc.sbuf_tensor(name, shape, dt))

    x = nc.dram_tensor("x", [S, D], F32, kind="ExternalInput")
    csA_d = nc.dram_tensor("csA", [128, S], F32, kind="ExternalInput")
    csB_d = nc.dram_tensor("csB", [128, S], F32, kind="ExternalInput")
    wqT = nc.dram_tensor("wqT", [D, D], F32, kind="ExternalInput")
    wkT = nc.dram_tensor("wkT", [D, D], F32, kind="ExternalInput")
    wvT = nc.dram_tensor("wvT", [D, D], F32, kind="ExternalInput")
    woT = nc.dram_tensor("woT", [D, D], F32, kind="ExternalInput")
    bqr = nc.dram_tensor("bqr", [128, 16], F32, kind="ExternalInput")
    bkr = nc.dram_tensor("bkr", [128, 16], F32, kind="ExternalInput")
    bvr_d = nc.dram_tensor("bvr", [128, 16], F32, kind="ExternalInput")
    bo_bc_d = nc.dram_tensor("bo_bc", [128, D], F32, kind="ExternalInput")
    gate_r = nc.dram_tensor("gate_r", [128, 128], F32, kind="ExternalInput")
    wi0 = nc.dram_tensor("wi0", [E, D, M], F32, kind="ExternalInput")
    wi1 = nc.dram_tensor("wi1", [E, D, M], F32, kind="ExternalInput")
    wof = nc.dram_tensor("wof", [E, M, D], F32, kind="ExternalInput")
    sb_sc_d = nc.dram_tensor("sb_scale", [128, NREST], F32, kind="ExternalInput")
    sb_bi_d = nc.dram_tensor("sb_bias", [128, NREST], F32, kind="ExternalInput")
    masks_d = nc.dram_tensor("masks", [128, 4 * 512], F32, kind="ExternalInput")
    ident_d = nc.dram_tensor("ident", [128, 128], F32, kind="ExternalInput")
    ones_d = nc.dram_tensor("ones_in", [2, 128], F32, kind="ExternalInput")
    constv = nc.dram_tensor("constv", [128, 8], F32, kind="ExternalInput")
    out = nc.dram_tensor("out", [ROWS, D], F32, kind="ExternalOutput")
    dbg_out = None
    if dbg is not None:
        dbg_out = nc.dram_tensor("dbg", list(dbg[1]), dbg[2], kind="ExternalOutput")

    ln1T_d = nc.dram_tensor("ln1T_d", [NQT, D, 512], BF16, kind="Internal")
    q_d = nc.dram_tensor("q_d", [H, 128, ROWS], BF16, kind="Internal")
    k_d = nc.dram_tensor("k_d", [H, 128, S], BF16, kind="Internal")
    v_d = nc.dram_tensor("v_d", [H, 128, 64, 128], BF16, kind="Internal")
    ln2T_d = nc.dram_tensor("ln2T_d", [4, D, 512], BF16, kind="Internal")
    xp_d = nc.dram_tensor("xp_d", [ROWS, D], F32, kind="Internal")
    c_d = nc.dram_tensor("c_d", [128, H * ROWS], BF16, kind="Internal")
    cw_d = nc.dram_tensor("cw_d", [128, 128], F32, kind="Internal")

    sch.sems(*DMA_SEMS)
    sch.sems("pe", "act", "dve")

    ident_bf = sb("ident_bf", [128, 128], BF16)
    ones_bf = sb("ones_bf", [128, 1], BF16)
    ones1f = sb("ones1f", [1, 128], F32)
    cw_sb = sb("cw_sb", [128, 16 * 8], F32)
    cst = sb("cst", [128, 8], F32)

    sch.begin()
    sch.dma("gpsimd", ident_bf[:, :], ident_d[:, :], inc="dmg")
    sch.dma("gpsimd", ones_bf[:, :], ones_d[0:1, :].rearrange("c p -> p c"), inc="dmg",
            allow_slow_non_contiguous=True)
    sch.dma("sync", ones1f[0:1, :], ones_d[1:2, :], inc="dmc")
    sch.dma("sync", cst[:, :], constv[:, :], inc="dmc")
    sch.drain()
    sch.emit("consts")

    p1s = contextlib.ExitStack()
    sb1 = lambda n, shp, dt: p1s.enter_context(nc.sbuf_tensor(n, shp, dt))
    ps1 = lambda n, shp, dt=F32: p1s.enter_context(nc.psum_tensor(n, shp, dt))
    wsb = sb1("p1_w", [128, 16 * 2048], BF16)
    lnT = [sb1(f"p1_lnT{i}", [128, 16 * 1024], BF16) for i in range(2)]
    xb = sb1("p1_x", [128, 2 * 2048], F32)
    lnbf = [sb1(f"p1_lnbf{i}", [128, 2048], BF16) for i in range(8)]
    scr = sb1("p1_scr", [128, 2048], F32)
    stat = sb1("p1_stat", [128, 16], F32)
    csAp = sb1("p1_csAp", [128, 1024], F32)
    csBp = sb1("p1_csBp", [128, 1024], F32)
    t1 = sb1("p1_t1", [128, 512], F32)
    t2 = sb1("p1_t2", [128, 512], F32)
    t3 = sb1("p1_t3", [128, 512], F32)
    stg = sb1("p1_stg", [128, 2 * 512], BF16)
    bcol = sb1("p1_b", [128, 16], F32)
    bvb = scr
    mm4 = [ps1(f"p1_mm{i}", [128, 512]) for i in range(4)]
    psT = [ps1(f"p1_psT{i}", [128, 128], BF16) for i in range(2)]

    def load_w(wdram):
        for dc in range(16):
            sch.dma("gpsimd", wsb[:, dc * 2048:(dc + 1) * 2048],
                    wdram[dc * 128:(dc + 1) * 128, :], inc="lw")
        return sch.cnt["lw"]

    def ln_stats(pair, i, st):
        row = pair * 1024 + i * 128
        sl = i % 2
        xslot = xb[:, sl * 2048:(sl + 1) * 2048]
        dxs = f"dx{sl}"
        w = []
        if len(st["norm"]) >= 2:
            w.append(("dve", st["norm"][-2]))
        sch.dma("sync", xslot, x[row:row + 128, :], waits=w, inc=dxs)
        ldx = (dxs, sch.cnt[dxs])
        sch.add("scalar", lambda e, xs=xslot: e.activation(scr[:, :], xs, AF.Copy, accum_out=stat[:, 0:1]),
                waits=[ldx, ("dve", sch.cnt["dve"])], inc="act")
        sch.add("scalar", lambda e, xs=xslot: e.activation(scr[:, :], xs, AF.Square, accum_out=stat[:, 1:2]),
                waits=[], inc="act")
        sch.add("vector", lambda e: e.tensor_scalar(stat[:, 2:3], stat[:, 0:1], 1.0 / 2048.0, None, ALU.mult),
                waits=["act"], inc="dve")
        sch.add("vector", lambda e: e.tensor_tensor(stat[:, 3:4], stat[:, 2:3], stat[:, 2:3], ALU.mult),
                waits=["dve"], inc="dve")
        sch.add("vector", lambda e: e.tensor_scalar(stat[:, 4:5], stat[:, 1:2], 1.0 / 2048.0, None, ALU.mult),
                waits=["dve"], inc="dve")
        sch.add("vector", lambda e: e.tensor_tensor(stat[:, 5:6], stat[:, 4:5], stat[:, 3:4], ALU.subtract),
                waits=["dve"], inc="dve")
        sch.add("scalar", lambda e: e.activation(stat[:, 6:7], stat[:, 5:6], AF.Sqrt, bias=cst[:, 0:1]),
                waits=["dve"], inc="act")
        sch.add("vector", lambda e: e.reciprocal(stat[:, 7:8], stat[:, 6:7]),
                waits=["act"], inc="dve")
        sch.add("vector", lambda e: e.scalar_tensor_tensor(stat[:, 8:9], stat[:, 2:3], -1.0, stat[:, 7:8], ALU.mult, ALU.mult),
                waits=["dve"], inc="dve")
        w = ["dve"]
        if i in st["evacT"]:
            w.append(("pe", st["evacT"][i]))
        lnslot = lnbf[i]
        sch.add("vector", lambda e, xs=xslot, lnslot=lnslot: e.tensor_scalar(lnslot[:, :], xs, stat[:, 7:8], stat[:, 8:9], ALU.mult, ALU.add),
                waits=w, inc="dve")
        st["norm"].append(sch.cnt["dve"])
        st["lnr"][i] = sch.cnt["dve"]

    def ln_transpose(pair, i, lnT_free_pe, st):
        lb = lnT[pair % 2]
        lnslot = lnbf[i]
        lnT_free_st = sch.cnt["stln"]
        for dc in range(16):
            bk = dc % 2
            w = [("dve", st["lnr"][i])]
            if ("ev", bk) in st["evacT"]:
                w.append(st["evacT"][("ev", bk)])
            sch.add("tensor", lambda e, bk=bk, dc=dc, lnslot=lnslot: e.transpose(
                    psT[bk][:, :], lnslot[:, dc * 128:(dc + 1) * 128], ident_bf[:, :]),
                    waits=w, inc="pe")
            ew = [("pe", sch.cnt["pe"]), ("stln", lnT_free_st)]
            if lnT_free_pe > 0:
                ew.append(("pe", lnT_free_pe))
            sch.add("scalar", lambda e, bk=bk, dc=dc, i=i, lb=lb: e.activation(
                    lb[:, dc * 1024 + i * 128: dc * 1024 + i * 128 + 128], psT[bk][:, :], AF.Copy),
                    waits=ew, inc="act")
            st["evacT"][("ev", bk)] = ("act", sch.cnt["act"])
        st["evacT"][i] = sch.cnt["pe"]
        if i == 7:
            done = ("act", sch.cnt["act"])
            for half in range(2):
                qt = 2 * pair + half
                src = lb.ap().rearrange("p (c t) -> p c t", t=1024)[:, :, half * 512:(half + 1) * 512]
                dst = ln1T_d.ap()[qt].rearrange("(c p) t -> p c t", p=128)
                sch.dma("scalar", dst, src, waits=[done], inc="stln")
            st["done"][pair] = done

    def load_lnT(pair, free_pe):
        st0 = sch.cnt["stln"]
        lb = lnT[pair % 2]
        sps = f"sp{pair % 2}"
        for half in range(2):
            qt = 2 * pair + half
            dst = lb.ap().rearrange("p (c t) -> p c t", t=1024)[:, :, half * 512:(half + 1) * 512]
            src = ln1T_d.ap()[qt].rearrange("(c p) t -> p c t", p=128)
            w = [("pe", free_pe), ("stln", st0)] if free_pe > 0 else [("stln", st0)]
            sch.dma("sync", dst, src, waits=w, inc=sps)
        return (sps, sch.cnt[sps])

    def rope_evac(bank, h, pair, half, out_ap, stg_hist, rope_done, tabs):
        sl = h % 2
        sts = f"st{sl}"
        c0 = half * 512
        pev = sch.cnt["pe"]
        w0 = [("pe", pev), ("csld", tabs)]
        if stg_hist.get(sl, 0) > 0:
            w0.append((sts, stg_hist[sl]))
        sch.add("vector", lambda e: e.scalar_tensor_tensor(t1[:, :], bank[:, :], bcol[:, h:h + 1], csAp[:, c0:c0 + 512], ALU.add, ALU.mult), waits=w0, inc="dve")
        sch.add("vector", lambda e: e.scalar_tensor_tensor(t2[:, :], bank[:, :], bcol[:, h:h + 1], csBp[:, c0:c0 + 512], ALU.add, ALU.mult), inc="dve")
        rope_done[id(bank)] = sch.cnt["dve"]
        sch.add("vector", lambda e: e.tensor_copy(t3[0:64, :], t2[64:128, :]), waits=["dve"], inc="dve")
        sch.add("vector", lambda e: e.tensor_copy(t3[64:128, :], t2[0:64, :]), inc="dve")
        sch.add("vector", lambda e, sl=sl: e.tensor_tensor(stg[:, sl * 512:(sl + 1) * 512], t1[:, :], t3[:, :], ALU.add), waits=["dve"], inc="dve")
        sch.dma("scalar", out_ap, stg[:, sl * 512:(sl + 1) * 512], waits=["dve"], inc=sts)
        stg_hist[sl] = sch.cnt[sts]

    def proj_dmajor(out_dram, pairs, compute_ln):
        st = {"norm": [], "evacT": {}, "lnr": {}, "done": {}}
        stg_hist, rope_done = {}, {}
        lw = sch.cnt["lw"]
        mm_done = {}
        rope_last = {}
        if compute_ln:
            for i in range(8):
                ln_stats(pairs[0], i, st)
                ln_transpose(pairs[0], i, 0, st)
            for i in range(8):
                if len(pairs) > 1:
                    ln_stats(pairs[1], i, st)
        for pi, pair in enumerate(pairs):
            cw0 = [("dve", rope_last.get(pi - 1, 0))] if rope_last.get(pi - 1, 0) > 0 else []
            sch.dma("sync", csAp[:, :], csA_d[:, pair * 1024:(pair + 1) * 1024], waits=cw0, inc="csld")
            sch.dma("sync", csBp[:, :], csB_d[:, pair * 1024:(pair + 1) * 1024], inc="csld")
            tabs = sch.cnt["csld"]
            if compute_ln:
                lnT_ready = st["done"][pair]
            else:
                lnT_ready = load_lnT(pair, mm_done.get(pi - 2, 0))
            lb = lnT[pair % 2]
            weave = compute_ln and (pi + 1 < len(pairs))
            for h in range(16):
                banks = [mm4[(h % 2) * 2], mm4[(h % 2) * 2 + 1]]
                for dc in range(16):
                    for half in range(2):
                        bank = banks[half]
                        w = []
                        if dc == 0:
                            w = [("lw", lw), lnT_ready]
                            if rope_done.get(id(bank), 0) > 0:
                                w.append(("dve", rope_done[id(bank)]))
                        sch.add("tensor", lambda e, bank=bank, dc=dc, h=h, half=half, lb=lb:
                                e.matmul(bank[:, :], wsb[:, dc * 2048 + h * 128: dc * 2048 + h * 128 + 128],
                                         lb[:, dc * 1024 + half * 512: dc * 1024 + (half + 1) * 512],
                                         start=(dc == 0), stop=(dc == 15), skip_group_check=True),
                                waits=w, inc=("pe" if (dc == 15 and half == 1) else None))
                for half in range(2):
                    col0 = pair * 1024 + half * 512
                    rope_evac(banks[half], h, pair, half, out_dram.ap()[h][:, col0:col0 + 512], stg_hist, rope_done, tabs)
                if weave:
                    if h < 8:
                        ln_transpose(pairs[pi + 1], h, mm_done.get(pi - 1, 0), st)
                    elif pi + 2 < len(pairs):
                        ln_stats(pairs[pi + 2], h - 8, st)
            mm_done[pi] = sch.cnt["pe"]
            rope_last[pi] = sch.cnt["dve"]

    if "k" in phases:
        sch.begin()
        load_w(wkT)
        sch.dma("sync", bcol[:, :], bkr[:, :], inc="csld")
        proj_dmajor(k_d, list(range(8)), True)
        sch.drain()
        sch.emit("p1k")

    if "q" in phases:
        sch.begin()
        load_w(wqT)
        sch.dma("sync", bcol[:, :], bqr[:, :], inc="csld")
        proj_dmajor(q_d, [0, 1], False)
        sch.drain()
        sch.emit("p1q")

    if "v" in phases:
        sch.begin()
        lwv = load_w(wvT)
        st_hist = 0
        vmm_done = {}
        for pair in range(8):
            lspill = load_lnT(pair, vmm_done.get(pair - 2, 0))
            lb = lnT[pair % 2]
            for ti in range(8):
                evd = sch.cnt["act"]
                for dc in range(16):
                    for dv in range(4):
                        w = []
                        if dc == 0 and dv == 0:
                            w = [("lw", lwv), lspill, ("act", evd)]
                        sch.add("tensor", lambda e, dc=dc, dv=dv, ti=ti, lb=lb:
                                e.matmul(mm4[dv][:, :],
                                         lb[:, dc * 1024 + ti * 128: dc * 1024 + ti * 128 + 128],
                                         wsb[:, dc * 2048 + dv * 512: dc * 2048 + (dv + 1) * 512],
                                         start=(dc == 0), stop=(dc == 15), skip_group_check=True),
                                waits=w, inc=("pe" if (dc == 15 and dv == 3) else None))
                pev = sch.cnt["pe"]
                chunk = pair * 8 + ti
                for dv in range(4):
                    w = [("pe", pev)]
                    if st_hist > 0:
                        w.append(("stv", st_hist))
                    sch.add("scalar", lambda e, dv=dv: e.activation(
                            lnbf[dv][:, 0:512], mm4[dv][:, :], AF.Copy),
                            waits=(w if dv == 0 else []), inc="act")
                evdone = sch.cnt["act"]
                for dv in range(4):
                    for qhh in range(4):
                        hh = dv * 4 + qhh
                        sch.dma("scalar", v_d.ap()[hh][:, chunk, :],
                                lnbf[dv][:, qhh * 128:(qhh + 1) * 128],
                                waits=[("act", evdone)], inc="stv")
                st_hist = sch.cnt["stv"]
            vmm_done[pair] = sch.cnt["pe"]
        sch.drain()
        sch.emit("p1v")

    p1s.close()

    catt = contextlib.ExitStack()
    C_all = catt.enter_context(nc.sbuf_tensor("C_all", [128, H * ROWS], BF16))
    if "att" in phases:
        with contextlib.ExitStack() as p2s:
            sb2 = lambda n, shp, dt: p2s.enter_context(nc.sbuf_tensor(n, shp, dt))
            ps2 = lambda n, shp, dt=F32: p2s.enter_context(nc.psum_tensor(n, shp, dt))
            kh = [sb2(f"p2_k{i}", [128, S], BF16) for i in range(2)]
            vh = [sb2(f"p2_v{i}", [128, S], BF16) for i in range(2)]
            qh = [sb2(f"p2_q{i}", [128, ROWS], BF16) for i in range(2)]
            pt_b = [sb2(f"p2_pt{i}", [128, 512], BF16) for i in range(3)]
            mk_sb = sb2("p2_masks", [128, 4 * 512], F32)
            sbsc = sb2("p2_sbsc", [128, NREST], F32)
            sbbi = sb2("p2_sbbi", [128, NREST], F32)
            rsum = sb2("p2_rsum", [1, 512], F32)
            rb = sb2("p2_rb", [128, 512], F32)
            ct = sb2("p2_ct", [128, 512], F32)
            bvc = sb2("p2_bvc", [128, 16], F32)
            psS = [ps2(f"p2_psS{i}", [128, 512]) for i in range(3)]
            psC = [ps2(f"p2_psC{i}", [128, 512]) for i in range(2)]
            psR = [ps2(f"p2_psR{i}", [128, 512]) for i in range(2)]
            psBC = ps2("p2_psBC", [128, 512])

            sch.begin()
            sch.dma("sync", mk_sb[:, :], masks_d[:, :], inc="dmc")
            sch.dma("sync", bvc[:, :], bvr_d[:, :], inc="dmc")
            sch.dma("sync", sbsc[:, :], sb_sc_d[:, :], inc="dmc")
            sch.dma("sync", sbbi[:, :], sb_bi_d[:, :], inc="dmc")
            tabs = sch.cnt["dmc"]
            kvq_done = {}
            cprev = {}
            ld = {}
            pend_epi = [None]

            def load_head(h):
                i = h % 2
                hs = f"hd{i}"
                w = []
                if kvq_done.get(h - 2, 0) > 0:
                    w = [("pe", kvq_done[h - 2])]
                sch.dma("sync", kh[i][:, :], k_d.ap()[h], waits=w, inc=hs)
                sch.dma("sync", vh[i][:, :], v_d.ap()[h].rearrange("p c d -> p (c d)"), inc=hs)
                sch.dma("sync", qh[i][:, :], q_d.ap()[h], inc=hs)
                return (hs, sch.cnt[hs])

            ld[0] = load_head(0)
            ld[1] = load_head(1)
            for h in range(16):
                i = h % 2
                for s in range(4):
                    g = (h * 4 + s) % 2
                    bC = psC[g]
                    bR = psR[g]
                    units = slot_units(s)
                    nu = len(units)
                    exp_done = {}
                    sv = {}

                    def emit_S(u):
                        kind, kt, aux = units[u]
                        bS = psS[u % 3]
                        if u < 3:
                            w = [ld[h], ("act", sch.cnt["act"])]
                        else:
                            w = [("act", exp_done[u - 3])]
                        sch.add("tensor", lambda e, bS=bS, kt=kt, i=i, s=s:
                                e.matmul(bS[:, :], kh[i][:, kt * 128:(kt + 1) * 128],
                                         qh[i][:, s * 512:(s + 1) * 512], start=True, stop=True,
                                         skip_group_check=True),
                                waits=w, inc="pe")
                        sv[u] = sch.cnt["pe"]

                    for uu in range(min(3, nu)):
                        emit_S(uu)
                    if pend_epi[0] is not None:
                        pend_epi[0]()
                        pend_epi[0] = None
                    for u, (kind, kt, aux) in enumerate(units):
                        bS = psS[u % 3]
                        if kind == "diag":
                            sch.add("vector", lambda e, bS=bS, aux=aux: e.tensor_tensor(
                                    bS[:, :], bS[:, :], mk_sb[:, aux * 512:(aux + 1) * 512], ALU.add),
                                    waits=[("pe", sv[u]), ("dmc", tabs)], inc="dve")
                            mw = [("dve", sch.cnt["dve"])]
                        else:
                            mw = [("pe", sv[u])]
                        ptb = pt_b[u % 3]
                        if kind == "rest":
                            sch.add("scalar", lambda e, bS=bS, ptb=ptb, aux=aux:
                                    e.activation(ptb[:, :], bS[:, :], AF.Exp,
                                                 bias=sbbi[:, aux:aux + 1], scale=sbsc[:, aux:aux + 1]),
                                    waits=mw + [("dmc", tabs)], inc="act")
                        else:
                            sch.add("scalar", lambda e, bS=bS, ptb=ptb:
                                    e.activation(ptb[:, :], bS[:, :], AF.Exp, bias=cst[:, 1:2], scale=cst[:, 2:3]),
                                    waits=mw, inc="act")
                        exp_done[u] = sch.cnt["act"]
                        wpv = [("act", exp_done[u])]
                        if u == 0 and cprev.get(g, 0) > 0:
                            wpv.append(("dve", cprev[g]))
                        sch.add("tensor", lambda e, ptb=ptb, kt=kt, i=i, bC=bC, first=(u == 0), last=(u == nu - 1):
                                e.matmul(bC[:, :], vh[i][:, kt * 128:(kt + 1) * 128], ptb[:, :],
                                         start=first, stop=last, skip_group_check=True),
                                waits=wpv, inc=None)
                        sch.add("tensor", lambda e, ptb=ptb, bR=bR, first=(u == 0), last=(u == nu - 1):
                                e.matmul(bR[0:1, :], ones_bf[:, :], ptb[:, :],
                                         start=first, stop=last, skip_group_check=True),
                                inc="pe")
                        if u + 3 < nu:
                            emit_S(u + 3)
                    pv_done = sch.cnt["pe"]

                    def epi(bC=bC, bR=bR, h=h, s=s, g=g, pv_done=pv_done):
                        sch.add("vector", lambda e: e.reciprocal(rsum[0:1, :], bR[0:1, :]),
                                waits=[("pe", pv_done)], inc="dve")
                        sch.add("tensor", lambda e: e.matmul(psBC[:, :], ones1f[0:1, :], rsum[0:1, :],
                                                             start=True, stop=True, skip_group_check=True),
                                waits=["dve"], inc="pe")
                        sch.add("vector", lambda e: e.tensor_copy(rb[:, :], psBC[:, :]),
                                waits=["pe"], inc="dve")
                        sch.add("vector", lambda e: e.tensor_tensor(
                                ct[:, :], bC[:, :], rb[:, :], ALU.mult),
                                waits=["dve"], inc="dve")
                        sch.add("vector", lambda e: e.tensor_scalar(
                                C_all[:, h * ROWS + s * 512: h * ROWS + (s + 1) * 512], ct[:, :], bvc[:, h:h + 1], None, ALU.add),
                                waits=["dve"], inc="dve")
                        cprev[g] = sch.cnt["dve"]
                    pend_epi[0] = epi
                kvq_done[h] = sch.cnt["pe"]
                if h + 2 < 16:
                    ld[h + 2] = load_head(h + 2)
            if pend_epi[0] is not None:
                pend_epi[0]()
                pend_epi[0] = None
            if dbg is not None and dbg[0] == "C_all":
                sch.dma("gpsimd", c_d.ap(), C_all.ap(), waits=[("dve", sch.cnt["dve"])], inc="dmg")
            sch.drain()
            sch.emit("p2")

    if "oproj" in phases:
        with contextlib.ExitStack() as p3s:
            sb3 = lambda n, shp, dt: p3s.enter_context(nc.sbuf_tensor(n, shp, dt))
            ps3 = lambda n, shp, dt=F32: p3s.enter_context(nc.psum_tensor(n, shp, dt))
            wosb = sb3("p3_wo", [128, 16 * 2048], BF16)
            ident_f = sb3("p3_identf", [128, 128], F32)
            xb3 = sb3("p3_x", [128, 2 * 2048], F32)
            xp3 = sb3("p3_xp", [128, 2048], F32)
            st3 = sb3("p3_stat", [128, 16], F32)
            ln3f_a = sb3("p3_lnfa", [128, 2048], F32)
            ln3f_b = sb3("p3_lnfb", [128, 2048], F32)
            tp3 = sb3("p3_tp", [128, 32], F32)
            lnTs = sb3("p3_lnTs", [128, 16 * 128], F32)
            lnTq = sb3("p3_lnTq", [128, 16 * 512], BF16)
            bob = sb3("p3_bob", [128, D], F32)
            gsb = sb3("p3_g", [128, 128], F32)
            psO = [ps3(f"p3_psO{i}", [128, 512]) for i in range(4)]
            psT3 = [ps3(f"p3_psT{i}", [128, 128]) for i in range(2)]
            psG = ps3("p3_psG", [128, 8])

            sch.begin()
            for dc in range(16):
                sch.dma("gpsimd", wosb[:, dc * 2048:(dc + 1) * 2048],
                        woT[dc * 128:(dc + 1) * 128, :], inc="lw")
            lwo = sch.cnt["lw"]
            sch.dma("sync", bob[:, :], bo_bc_d[:, :], inc="dmc")
            sch.dma("sync", gsb[:, :], gate_r[:, :], inc="dmc")
            sch.dma("sync", ident_f[:, :], ident_d[:, :], inc="dmc")
            tb3 = sch.cnt["dmc"]
            xread, xpst, lnq_st, tev = {}, {}, 0, {}
            ln3f = [ln3f_a, ln3f_b]
            lnfd, gstate = {}, {}
            for tt in range(17):
                if tt < 16:
                    sl = tt % 2
                    xslot = xb3[:, sl * 2048:(sl + 1) * 2048]
                    dxs = f"dx{sl}"
                    w = []
                    if xread.get(tt - 2, 0) > 0:
                        w = [("dve", xread[tt - 2])]
                    sch.dma("sync", xslot, x[tt * 128:(tt + 1) * 128, :], waits=w, inc=dxs)
                    ldx = (dxs, sch.cnt[dxs])
                    evd = sch.cnt["dve"]
                    for ic in range(16):
                        for oc in range(4):
                            w = []
                            if ic == 0 and oc == 0:
                                w = [("lw", lwo), ("dve", evd)]
                            sch.add("tensor", lambda e, ic=ic, oc=oc, tt=tt:
                                    e.matmul(psO[oc][:, :],
                                             C_all[:, ic * ROWS + tt * 128: ic * ROWS + (tt + 1) * 128],
                                             wosb[:, ic * 2048 + oc * 512: ic * 2048 + (oc + 1) * 512],
                                             start=(ic == 0), stop=(ic == 15), skip_group_check=True),
                                    waits=w, inc=("pe" if (ic == 15 and oc == 3) else None))
                    odone = sch.cnt["pe"]
                    for oc in range(4):
                        w = [("pe", odone), ldx, ("dmc", tb3)]
                        if xpst.get(tt - 1, 0) > 0:
                            w.append(("stx", xpst[tt - 1]))
                        sch.add("vector", lambda e, oc=oc, xslot=xslot: e.tensor_tensor(
                                xp3[:, oc * 512:(oc + 1) * 512], psO[oc][:, :], xslot[:, oc * 512:(oc + 1) * 512], ALU.add),
                                waits=(w if oc == 0 else []), inc="dve")
                        sch.add("vector", lambda e, oc=oc: e.tensor_tensor(
                                xp3[:, oc * 512:(oc + 1) * 512], xp3[:, oc * 512:(oc + 1) * 512], bob[:, oc * 512:(oc + 1) * 512], ALU.add),
                                waits=["dve"], inc="dve")
                    xread[tt] = sch.cnt["dve"]
                    xr = sch.cnt["dve"]
                    sch.dma("scalar", xp_d[tt * 128:(tt + 1) * 128, :], xp3[:, :], waits=[("dve", xr)], inc="stx")
                    xpst[tt] = sch.cnt["stx"]
                    lf = ln3f[tt % 2]
                    sch.add("scalar", lambda e, lf=lf: e.activation(lf[:, :], xp3[:, :], AF.Copy, accum_out=st3[:, 0:1]),
                            waits=[("dve", xr), ("dve", tev.get(tt - 1, 0)), ("pe", sch.cnt["pe"])], inc="act")
                    sch.add("scalar", lambda e, lf=lf: e.activation(lf[:, :], xp3[:, :], AF.Square, accum_out=st3[:, 1:2]),
                            waits=[], inc="act")
                    sch.add("vector", lambda e: e.tensor_scalar(st3[:, 2:3], st3[:, 0:1], 1.0 / 2048.0, None, ALU.mult), waits=["act"], inc="dve")
                    sch.add("vector", lambda e: e.tensor_tensor(st3[:, 3:4], st3[:, 2:3], st3[:, 2:3], ALU.mult), waits=["dve"], inc="dve")
                    sch.add("vector", lambda e: e.tensor_scalar(st3[:, 4:5], st3[:, 1:2], 1.0 / 2048.0, None, ALU.mult), waits=["dve"], inc="dve")
                    sch.add("vector", lambda e: e.tensor_tensor(st3[:, 5:6], st3[:, 4:5], st3[:, 3:4], ALU.subtract), waits=["dve"], inc="dve")
                    sch.add("scalar", lambda e: e.activation(st3[:, 6:7], st3[:, 5:6], AF.Sqrt, bias=cst[:, 0:1]), waits=["dve"], inc="act")
                    sch.add("vector", lambda e: e.reciprocal(st3[:, 7:8], st3[:, 6:7]), waits=["act"], inc="dve")
                    sch.add("vector", lambda e: e.scalar_tensor_tensor(st3[:, 8:9], st3[:, 2:3], -1.0, st3[:, 7:8], ALU.mult, ALU.mult), waits=["dve"], inc="dve")
                    sch.add("vector", lambda e, lf=lf: e.tensor_scalar(lf[:, :], xp3[:, :], st3[:, 7:8], st3[:, 8:9], ALU.mult, ALU.add), waits=["dve"], inc="dve")
                    lnfd[tt] = ("dve", sch.cnt["dve"])
                pt_ = tt - 1
                if pt_ < 0:
                    continue
                lf = ln3f[pt_ % 2]
                for dc in range(16):
                    bk = dc % 2
                    w = [lnfd[pt_]]
                    if tev.get(("b", bk), 0) > 0:
                        w.append(("dve", tev[("b", bk)]))
                    if tev.get(("a", bk), 0) > 0:
                        w.append(("act", tev[("a", bk)]))
                    sch.add("tensor", lambda e, bk=bk, dc=dc, lf=lf: e.transpose(psT3[bk][:, :], lf[:, dc * 128:(dc + 1) * 128], ident_f[:, :]),
                            waits=w, inc="pe")
                    pev = sch.cnt["pe"]
                    sch.add("scalar", lambda e, bk=bk, dc=dc: e.activation(lnTs[:, dc * 128:(dc + 1) * 128], psT3[bk][:, :], AF.Copy),
                            waits=[("pe", pev)], inc="act")
                    tev[("a", bk)] = sch.cnt["act"]
                    w = [("act", sch.cnt["act"])]
                    if lnq_st > 0:
                        w.append(("stln", lnq_st))
                    sch.add("vector", lambda e, dc=dc, pt_=pt_: e.tensor_copy(
                            lnTq[:, dc * 512 + (pt_ % 4) * 128: dc * 512 + (pt_ % 4) * 128 + 128],
                            lnTs[:, dc * 128:(dc + 1) * 128]),
                            waits=w, inc="dve")
                    tev[("b", bk)] = sch.cnt["dve"]
                tev[pt_] = sch.cnt["dve"]
                tdone_act = sch.cnt["act"]
                for dc in range(16):
                    sch.add("tensor", lambda e, dc=dc: e.matmul(psG[:, :], lnTs[:, dc * 128:(dc + 1) * 128],
                                                                gsb[:, dc * 8:(dc + 1) * 8],
                                                                start=(dc == 0), stop=(dc == 15), skip_group_check=True),
                            waits=([("act", tdone_act), ("dve", sch.cnt["dve"])] if dc == 0 else []),
                            inc=("pe" if dc == 15 else None))
                gd = sch.cnt["pe"]
                sch.add("vector", lambda e: e.reduce_max(st3[:, 9:10], psG[:, :], axis=AX.X), waits=[("pe", gd)], inc="dve")
                sch.add("vector", lambda e: e.tensor_scalar(tp3[:, 0:8], psG[:, :], st3[:, 9:10], None, ALU.is_ge), waits=["dve"], inc="dve")
                sch.add("vector", lambda e: e.scalar_tensor_tensor(tp3[:, 8:16], tp3[:, 0:8], cst[:, 3:4], psG[:, :], ALU.mult, ALU.add), waits=["dve"], inc="dve")
                sch.add("vector", lambda e: e.reduce_max(st3[:, 10:11], tp3[:, 8:16], axis=AX.X), waits=["dve"], inc="dve")
                sch.add("vector", lambda e: e.tensor_scalar(tp3[:, 16:24], tp3[:, 8:16], st3[:, 10:11], None, ALU.is_ge), waits=["dve"], inc="dve")
                sch.add("vector", lambda e: e.tensor_tensor(st3[:, 11:12], st3[:, 10:11], st3[:, 9:10], ALU.subtract), waits=["dve"], inc="dve")
                sch.add("scalar", lambda e: e.activation(st3[:, 12:13], st3[:, 11:12], AF.Sigmoid), waits=["dve"], inc="act")
                sch.add("vector", lambda e: e.tensor_scalar(st3[:, 13:14], st3[:, 12:13], -1.0, 1.0, ALU.mult, ALU.add), waits=["act"], inc="dve")
                sch.add("vector", lambda e: e.tensor_scalar(tp3[:, 24:32], tp3[:, 0:8], st3[:, 13:14], None, ALU.mult), waits=["dve"], inc="dve")
                sch.add("vector", lambda e, pt_=pt_: e.scalar_tensor_tensor(
                        cw_sb[:, pt_ * 8:(pt_ + 1) * 8], tp3[:, 16:24], st3[:, 12:13], tp3[:, 24:32], ALU.mult, ALU.add),
                        waits=["dve"], inc="dve")
                if pt_ % 4 == 3:
                    qt = pt_ // 4
                    src = lnTq.ap().rearrange("p (c t) -> p c t", t=512)
                    dst = ln2T_d.ap()[qt].rearrange("(c p) t -> p c t", p=128)
                    sch.dma("scalar", dst, src, waits=[("dve", sch.cnt["dve"])], inc="stln")
                    lnq_st = sch.cnt["stln"]
            sch.drain()
            sch.emit("p3")
    catt.close()

    if "ffn" in phases:
        with contextlib.ExitStack() as p4s:
            sb4 = lambda n, shp, dt: p4s.enter_context(nc.sbuf_tensor(n, shp, dt))
            ps4 = lambda n, shp, dt=F32: p4s.enter_context(nc.psum_tensor(n, shp, dt))
            ln2h = sb4("p4_ln2", [128, 16 * 1024], BF16)
            wbuf = sb4("p4_w", [128, 32 * 1024], BF16)
            hbuf = sb4("p4_h", [128, 16 * 1024], BF16)
            acc = sb4("p4_acc", [128, 8 * 2048], F32)
            silu_t = sb4("p4_silu", [128, 512], BF16)
            psF = [ps4(f"p4_psF{i}", [128, 512]) for i in range(8)]

            sch.begin()
            for half in range(2):
                for qi in range(2):
                    qt = half * 2 + qi
                    dst = ln2h.ap().rearrange("p (c t) -> p c t", t=1024)[:, :, qi * 512:(qi + 1) * 512]
                    src = ln2T_d.ap()[qt].rearrange("(c p) t -> p c t", p=128)
                    sch.dma("sync", dst, src, waits=[("pe", sch.cnt["pe"])], inc="spill")
                for t8 in range(8):
                    row = half * 1024 + t8 * 128
                    sch.dma("sync", acc[:, t8 * 2048:(t8 + 1) * 2048], xp_d[row:row + 128, :],
                            waits=[("sto", sch.cnt["sto"]), ("dve", sch.cnt["dve"])], inc="spill")
                base = sch.cnt["spill"]

                ring_free = {-2: sch.cnt["pe"], -1: sch.cnt["pe"]}

                def load_wi(it, wdram, e_, mh):
                    slot = it % 2
                    wrs = f"wr{slot}"
                    w = [("pe", ring_free[it - 2])] if ring_free.get(it - 2, 0) > 0 else []
                    for dc in range(16):
                        sch.dma("gpsimd", wbuf[:, slot * 16384 + dc * 1024: slot * 16384 + (dc + 1) * 1024],
                                wdram.ap()[e_][dc * 128:(dc + 1) * 128, mh * 1024:(mh + 1) * 1024],
                                waits=(w if dc == 0 else []), inc=wrs)
                    return (wrs, sch.cnt[wrs])

                def load_wo(it, e_, mh):
                    slot = it % 2
                    wrs = f"wr{slot}"
                    w = [("pe", ring_free[it - 2])] if ring_free.get(it - 2, 0) > 0 else []
                    for mD in range(8):
                        m0 = mh * 8 + mD
                        sch.dma("gpsimd", wbuf[:, slot * 16384 + mD * 2048: slot * 16384 + (mD + 1) * 2048],
                                wof.ap()[e_][m0 * 128:(m0 + 1) * 128, :],
                                waits=(w if mD == 0 else []), inc=wrs)
                    return (wrs, sch.cnt[wrs])

                plan = []
                for e_ in range(E):
                    plan += [("a0", e_, 0), ("a0", e_, 1), ("a1", e_, 0), ("a1", e_, 1),
                             ("b", e_, 0), ("b", e_, 1)]
                for it, (kindp, e_, mh) in enumerate(plan):
                    if kindp == "a0":
                        lw_t = load_wi(it, wi0, e_, mh)
                    elif kindp == "a1":
                        lw_t = load_wi(it, wi1, e_, mh)
                    else:
                        lw_t = load_wo(it, e_, mh)
                    slot = it % 2
                    if kindp in ("a0", "a1"):
                        hev = {}
                        for mc in range(8):
                            m_abs = mh * 8 + mc
                            bks = [psF[(mc % 4) * 2], psF[(mc % 4) * 2 + 1]]
                            w = [lw_t, ("spill", base)]
                            if mc == 0:
                                w += [("act", sch.cnt["act"]), ("dve", sch.cnt["dve"])]
                            if mc >= 4:
                                w.append(("act" if kindp == "a0" else "dve",
                                          hev.get((mc % 4), 0)))
                            for dc in range(16):
                                for tc in range(2):
                                    sch.add("tensor", lambda e, bank=bks[tc], dc=dc, mc=mc, tc=tc, slot=slot:
                                            e.matmul(bank[:, :],
                                                     wbuf[:, slot * 16384 + dc * 1024 + mc * 128: slot * 16384 + dc * 1024 + mc * 128 + 128],
                                                     ln2h[:, dc * 1024 + tc * 512: dc * 1024 + (tc + 1) * 512],
                                                     start=(dc == 0), stop=(dc == 15), skip_group_check=True),
                                            waits=(w if (dc == 0 and tc == 0) else []),
                                            inc=("pe" if (dc == 15 and tc == 1) else None))
                            hd = sch.cnt["pe"]
                            for tc in range(2):
                                col = m_abs * 1024 + tc * 512
                                if kindp == "a0":
                                    sch.add("scalar", lambda e, bank=bks[tc], col=col: e.activation(
                                            hbuf[:, col:col + 512], bank[:, :], AF.Copy),
                                            waits=[("pe", hd)], inc="act")
                                    hev[mc % 4] = sch.cnt["act"]
                                else:
                                    sch.add("scalar", lambda e, col=col: e.activation(
                                            silu_t[:, :], hbuf[:, col:col + 512], AF.Silu),
                                            waits=[("pe", hd), ("dve", sch.cnt["dve"])], inc="act")
                                    sch.add("vector", lambda e, bank=bks[tc], col=col: e.tensor_tensor(
                                            hbuf[:, col:col + 512], silu_t[:, :], bank[:, :], ALU.mult),
                                            waits=["act"], inc="dve")
                                    hev[mc % 4] = sch.cnt["dve"]
                        ring_free[it] = sch.cnt["pe"]
                    else:
                        bev = {}
                        for t8 in range(8):
                            w0 = [lw_t, ("dve", bev.get(t8 % 2, sch.cnt["dve"]))]
                            if t8 == 0:
                                w0.append(("dve", sch.cnt["dve"]))
                            bset = [psF[(t8 % 2) * 4 + k] for k in range(4)]
                            for mD in range(8):
                                for oc in range(4):
                                    sch.add("tensor", lambda e, t8=t8, mD=mD, oc=oc, slot=slot, mh=mh, bset=bset:
                                            e.matmul(bset[oc][:, :],
                                                     hbuf[:, (mh * 8 + mD) * 1024 + t8 * 128: (mh * 8 + mD) * 1024 + t8 * 128 + 128],
                                                     wbuf[:, slot * 16384 + mD * 2048 + oc * 512: slot * 16384 + mD * 2048 + (oc + 1) * 512],
                                                     start=(mD == 0), stop=(mD == 7), skip_group_check=True),
                                            waits=(w0 if (mD == 0 and oc == 0) else []),
                                            inc=("pe" if (mD == 7 and oc == 3) else None))
                            bd = sch.cnt["pe"]
                            for oc in range(4):
                                sch.add("vector", lambda e, t8=t8, oc=oc, e_=e_, half=half, bset=bset: e.scalar_tensor_tensor(
                                        acc[:, t8 * 2048 + oc * 512: t8 * 2048 + (oc + 1) * 512],
                                        bset[oc][:, :],
                                        cw_sb[:, (half * 8 + t8) * 8 + e_:(half * 8 + t8) * 8 + e_ + 1],
                                        acc[:, t8 * 2048 + oc * 512: t8 * 2048 + (oc + 1) * 512],
                                        ALU.mult, ALU.add),
                                        waits=([("pe", bd)] if oc == 0 else []), inc="dve")
                            bev[t8 % 2] = sch.cnt["dve"]
                        ring_free[it] = sch.cnt["pe"]
                accd = sch.cnt["dve"]
                for t8 in range(8):
                    row = half * 1024 + t8 * 128
                    sch.dma("scalar", out[row:row + 128, :], acc[:, t8 * 2048:(t8 + 1) * 2048],
                            waits=[("dve", accd)], inc="sto")
            sch.drain()
            sch.emit("p4")

    if dbg is not None and dbg[0] == "cw":
        sch.begin()
        sch.dma("gpsimd", cw_d.ap(), cw_sb.ap(), inc="dmg")
        sch.dma("gpsimd", dbg_out.ap(), cw_d.ap(), waits=[("dmg", sch.cnt["dmg"])], inc="dmg")
        sch.wait_only("gpsimd", [("dmg", sch.cnt["dmg"])])
        sch.emit("dbgblk")
        dbg = None
    if dbg is not None:
        name = dbg[0]
        srcs = {"ln1T_d": ln1T_d, "q_d": q_d, "k_d": k_d, "v_d": v_d,
                "ln2T_d": ln2T_d, "xp_d": xp_d, "C_all": c_d}
        src = srcs[name]
        sch.begin()
        sch.dma("gpsimd", dbg_out.ap(), src.ap(), inc="dmg")
        sch.wait_only("gpsimd", [("dmg", sch.cnt["dmg"])])
        sch.emit("dbgblk")

    return nc



def _rope_tables():
    inv = (1.0 / (10000.0 ** (np.arange(0, DH, 2, dtype=np.float32) / np.float32(DH)))).astype(np.float32)
    t = np.arange(S, dtype=np.float32)
    fr = t[None, :] * inv[:, None]
    return np.cos(fr).astype(np.float32), np.sin(fr).astype(np.float32)


def core_perm(core):
    j = core % 4
    chunks = own_chunks(j)
    own = np.concatenate([np.arange(c * 512, (c + 1) * 512) for c in chunks])
    rest = np.setdiff1d(np.arange(S), own)
    return np.concatenate([own, rest]), chunks


def make_inputs(core, full):
    b = core // 4
    perm, chunks = core_perm(core)

    cos, sin = _rope_tables()
    csA = np.concatenate([cos[:, perm], cos[:, perm]], axis=0)
    csB = np.concatenate([sin[:, perm], -sin[:, perm]], axis=0)

    n_s = [4 * (chunks[s] - s) for s in range(4)]
    sc = np.zeros(NREST, np.float32)
    bi = np.zeros(NREST, np.float32)
    for s in range(4):
        for r in range(NMAX[s]):
            u = CUM[s] + r
            if r < n_s[s]:
                sc[u] = SCALE
                bi[u] = -CEXP
            else:
                sc[u] = 0.0
                bi[u] = -100.0
    sb_scale = np.ascontiguousarray(np.broadcast_to(sc, (128, NREST)))
    sb_bias = np.ascontiguousarray(np.broadcast_to(bi, (128, NREST)))

    mk = np.zeros((128, 4, 512), np.float32)
    kk = np.arange(128)[:, None]
    qq = np.arange(512)[None, :]
    for dk in range(4):
        mk[:, dk, :] = np.where(qq >= dk * 128 + kk, 0.0, -1e9)
    masks = np.ascontiguousarray(mk.reshape(128, 4 * 512))

    d = {
        "x": np.ascontiguousarray(full["L0_attn_hidden_states"][b][perm]),
        "csA": np.ascontiguousarray(csA),
        "csB": np.ascontiguousarray(csB),
        "wqT": np.ascontiguousarray(full["L0_attn_Wq"].T),
        "wkT": np.ascontiguousarray(full["L0_attn_Wk"].T),
        "wvT": np.ascontiguousarray(full["L0_attn_Wv"].T),
        "woT": np.ascontiguousarray(full["L0_attn_Wo"].T),
        "bqr": np.ascontiguousarray(full["L0_attn_bq"].reshape(16, 128).T),
        "bkr": np.ascontiguousarray(full["L0_attn_bk"].reshape(16, 128).T),
        "bvr": np.ascontiguousarray(full["L0_attn_bv"].reshape(16, 128).T),
        "bo_bc": np.ascontiguousarray(np.broadcast_to(full["L0_attn_bo"], (128, D))),
        "gate_r": np.ascontiguousarray(
            full["L0_ffn_gate_kernel"].reshape(16, 128, 8).transpose(1, 0, 2).reshape(128, 128)),
        "wi0": np.ascontiguousarray(full["L0_ffn_wi_0"]),
        "wi1": np.ascontiguousarray(full["L0_ffn_wi_1"]),
        "wof": np.ascontiguousarray(full["L0_ffn_wo"]),
        "sb_scale": sb_scale,
        "sb_bias": sb_bias,
        "masks": masks,
        "ident": np.eye(128, dtype=np.float32),
        "ones_in": np.ones((2, 128), np.float32),
        "constv": np.ascontiguousarray(np.broadcast_to(
            np.array([EPS, -CEXP, SCALE, -1e9, 1.0 / 2048.0, 0, 0, 0], np.float32), (128, 8))),
    }
    return d


def kernel(**inputs):
    full = {k: np.asarray(v) for k, v in inputs.items()}
    nc = build_program()
    in_maps = [make_inputs(core, full) for core in range(8)]
    res = run_bass_kernel_spmd(nc, in_maps, core_ids=list(range(8)))
    outp = np.zeros((B, S, D), np.float32)
    for core in range(8):
        b = core // 4
        perm, _ = core_perm(core)
        outp[b][perm[:ROWS]] = res.results[core]["out"]
    return outp
